# version 06
# baseline (speedup 1.0000x reference)
import numpy as np
import concourse.bass as bass
import concourse.mybir as mybir
from concourse.bass_utils import run_bass_kernel_spmd
from contextlib import ExitStack

F32 = mybir.dt.float32
BF16 = mybir.dt.bfloat16
I32 = mybir.dt.int32
AF = mybir.ActivationFunctionType
ALU = mybir.AluOpType
AX = mybir.AxisListType

B_, T_, C_ = 4, 4096, 1024
H_ = 16
DFF_ = 4096
NQ = 2048
NPOS = 16
NCH = 32
C_P = [32 - 2 * p for p in range(NPOS)]
N_SLOTS = [sum(1 for p in range(NPOS) if C_P[p] - 2 >= c) for c in range(31)]
TWO_PI = float(2.0 * np.pi)
INV_SQ96 = float(1.0 / np.sqrt(96.0))
SQ2 = float(np.sqrt(2.0))
NEG_BIG = -30000.0

TILES_A = sorted([j for j in range(32) if j % 4 in (0, 3)], reverse=True)
TILES_B = sorted([j for j in range(32) if j % 4 in (1, 2)], reverse=True)
for _tiles in (TILES_A, TILES_B):
    for _p, _j in enumerate(_tiles):
        assert _j in (C_P[_p] - 1, C_P[_p] - 2) and C_P[_p] - 2 >= _j - 1
assert sorted(TILES_A + TILES_B) == list(range(32))

GROUPS = []
for _c in range(31):
    _n = N_SLOTS[_c]
    _p0 = 0
    while _p0 < _n:
        _ln = min(8, _n - _p0)
        _pad = (_n - 1) if (_c % 2 == 0 and _p0 <= _n - 1 < _p0 + _ln) else None
        GROUPS.append((0, _c, _p0, _ln, _pad))
        _p0 += _ln
GROUPS.append((1, -1, 0, 8, None))
GROUPS.append((1, -1, 8, 8, None))
NGRP = len(GROUPS)


class Prog:

    SECT = {"t": "tensor", "v": "vector", "s": "scalar", "y": "sync", "g": "gpsimd"}

    NDSEM = 48

    def __init__(self, nc, ctx):
        self.nc = nc
        self.sems = {e: ctx.enter_context(nc.semaphore("sem_" + e)) for e in self.SECT}
        self.cnt = dict.fromkeys(self.SECT, 0)
        self.pend = {e: [] for e in self.SECT}
        self.seen = {}
        self.steps = []
        self.dsems = [ctx.enter_context(nc.semaphore(f"dq{i}")) for i in range(self.NDSEM)]
        self.duse = [0] * self.NDSEM
        self.dnext = 0

    def emit(self, e, fn, deps=(), kind="op", incs=1):
        waits = []
        for d in deps:
            if d is None:
                continue
            pe, pv = self.steps[d]
            if pe == e:
                continue
            key = (e, pe)
            if self.seen.get(key, 0) >= pv:
                continue
            self.seen[key] = pv
            waits.append(((self.sems[pe] if not isinstance(pe, tuple)
                           else self.dsems[pe[1]]), pv))
        sid = len(self.steps)
        if kind == "dma":
            i = self.dnext
            self.dnext = (self.dnext + 1) % self.NDSEM
            self.duse[i] += 1
            self.steps.append((("d", i), 16 * self.duse[i]))
            self.pend[e].append((waits, fn, kind, self.dsems[i]))
        else:
            self.cnt[e] += 1
            self.steps.append((e, self.cnt[e]))
            self.pend[e].append((waits, fn, kind, None))
        return sid

    def dma(self, e, out, in_):
        def fn(eng, sem, out=out, in_=in_):
            eng.dma_start(out, in_).then_inc(sem, 16)

        return fn

    def gchk(self, deps=()):
        def fn(g):
            g.drain(fusable=False)
            return g.nop()
        return self.emit("g", fn, deps=deps, kind="op")

    def drain(self, e):
        pairs = [(self.dsems[i], 16 * self.duse[i])
                 for i in range(self.NDSEM) if self.duse[i] > 0]

        def fn(eng, pairs=pairs):
            for sem, val in pairs:
                eng.wait_ge(sem, val)
            return eng.nop()

        return self.emit(e, fn, kind="op")

    def flush(self, block, drain=("y", "g", "s")):
        for e in drain:
            if self.pend[e]:
                if e == "g":
                    self.gchk()
                self.drain(e)
        for e in self.SECT:
            lst = self.pend[e]
            if not lst:
                continue
            sem = self.sems[e]

            def body(eng, lst=list(lst), sem=sem, e=e):
                for waits, fn, kind, dsem in lst:
                    for (sh, v) in waits:
                        eng.wait_ge(sh, v)
                    if kind == "op" and e in ("v", "s"):
                        eng.drain(fusable=False)
                    if kind == "dma":
                        fn(eng, dsem)
                    else:
                        fn(eng).then_inc(sem, 1)

            getattr(block, self.SECT[e])(body)
            self.pend[e] = []


class _StopBuild(Exception):
    pass


def build_nc(upto=99, taps=(), p2sub=3):
    nc = bass.Bass("TRN2", target_bir_lowering=False)
    _dt0 = nc.dram_tensor

    def dt(name, shape, dtype, kind=None):
        if kind is None:
            kind = "ExternalOutput" if name in taps else "Internal"
        return _dt0(name, shape, dtype, kind=kind)

    xfull = dt("xfull", [T_, C_], F32, kind="ExternalInput").ap()
    xq = dt("xq", [NQ, C_], F32, kind="ExternalInput").ap()
    W_DQ = dt("W_DQ", [C_, DFF_], F32, kind="ExternalInput").ap()
    W_UQ = dt("W_UQ", [DFF_, C_], F32, kind="ExternalInput").ap()
    W_QR = dt("W_QR", [DFF_, 512], F32, kind="ExternalInput").ap()
    W_DKV = dt("W_DKV", [C_, 256], F32, kind="ExternalInput").ap()
    W_UK = dt("W_UK", [256, C_], F32, kind="ExternalInput").ap()
    W_UV = dt("W_UV", [256, C_], F32, kind="ExternalInput").ap()
    WKRpt = dt("WKRpt", [128, 8, 32], F32, kind="ExternalInput").ap()
    W_O = dt("W_O", [C_, C_], F32, kind="ExternalInput").ap()
    W1 = dt("W1", [C_, DFF_], F32, kind="ExternalInput").ap()
    W2 = dt("W2", [DFF_, C_], F32, kind="ExternalInput").ap()
    b1pt = dt("b1pt", [128, 32], F32, kind="ExternalInput").ap()
    thpt = dt("thpt", [128, 32], F32, kind="ExternalInput").ap()
    wrpt = dt("wrpt", [128, 32], F32, kind="ExternalInput").ap()
    wipt = dt("wipt", [128, 32], F32, kind="ExternalInput").ap()
    b2row = dt("b2row", [1, C_], F32, kind="ExternalInput").ap()
    identf = dt("identf", [128, 128], F32, kind="ExternalInput").ap()
    tri8 = dt("tri8", [128, 1024], F32, kind="ExternalInput").ap()
    padbin = dt("padbin", [128, 16], F32, kind="ExternalInput").ap()
    trowq = dt("trowq", [32, NQ], F32, kind="ExternalInput").ap()
    trowf = dt("trowf", [32, T_], F32, kind="ExternalInput").ap()
    jcol = dt("jcol", [32, 1], F32, kind="ExternalInput").ap()

    out = dt("out", [NQ, C_], F32, kind="ExternalOutput").ap()

    wdqT_s = dt("wdqT_s", [DFF_, C_], BF16).ap()
    wu_s = dt("wu_s", [DFF_, 1536], BF16).ap()
    w1_s = dt("w1_s", [C_, DFF_], BF16).ap()
    w2_s = dt("w2_s", [DFF_, C_], BF16).ap()
    kall = dt("kall", [H_, 97, T_], BF16).ap()
    kdall = dt("kdall", [H_, 97, NQ], BF16).ap()
    vall = dt("vall", [H_, 128, 8, 260], BF16).ap()
    vdall = dt("vdall", [H_, 128, 4, 260], BF16).ap()
    qall = dt("qall", [H_, 97, NQ], BF16).ap()
    ffn_s = dt("ffn_s", [NQ, C_], F32).ap()
    denr_s = dt("denr_s", [16, NQ], F32).ap()
    denh_s = dt("denh_s", [16, NQ], F32).ap()

    ctx = ExitStack()
    try:
      with ctx:
        e = ctx.enter_context
        P = Prog(nc, ctx)

        identb = e(nc.sbuf_tensor("identb", [128, 128], BF16))
        tri8b = e(nc.sbuf_tensor("tri8b", [128, 1024], F32))
        padb = e(nc.sbuf_tensor("padb", [128, 16], F32))
        b2rep = e(nc.sbuf_tensor("b2rep", [128, C_], F32))
        b1c = e(nc.sbuf_tensor("b1c", [128, 32], F32))
        th2pi = e(nc.sbuf_tensor("th2pi", [128, 32], F32))
        thb1 = e(nc.sbuf_tensor("thb1", [128, 32], F32))
        kq = e(nc.sbuf_tensor("kq", [128, 32], F32))
        ksin = e(nc.sbuf_tensor("ksin", [128, 32], F32))
        kcos = e(nc.sbuf_tensor("kcos", [128, 32], F32))
        sinb = e(nc.sbuf_tensor("sinb", [128, 32], F32))
        cosb = e(nc.sbuf_tensor("cosb", [128, 32], F32))
        scfull = e(nc.sbuf_tensor("scfull", [32, T_], F32))
        scqk = e(nc.sbuf_tensor("scqk", [32, NQ], F32))
        scqrep = e(nc.sbuf_tensor("scqrep", [128, NQ], F32))
        epsc = e(nc.sbuf_tensor("epsc", [128, 1], F32))

        ctx12 = ExitStack()
        e12 = ctx12.enter_context
        xnqT = e12(nc.sbuf_tensor("xnqT", [128, 8, NQ], BF16))
        weff = e12(nc.sbuf_tensor("weff", [128, 8, 1536], BF16))

        with ExitStack() as c0, nc.Block("p0") as blk:
            e0 = c0.enter_context
            thc = e0(nc.sbuf_tensor("thc", [128, 32], F32))
            wrc = e0(nc.sbuf_tensor("wrc", [128, 32], F32))
            wic = e0(nc.sbuf_tensor("wic", [128, 32], F32))
            jc = e0(nc.sbuf_tensor("jc", [32, 1], F32))
            invc = e0(nc.sbuf_tensor("invc", [32, 1], F32))
            trqT = e0(nc.sbuf_tensor("trqT", [32, NQ], F32))
            trfT = e0(nc.sbuf_tensor("trfT", [32, T_], F32))
            sq32 = e0(nc.sbuf_tensor("sq32", [32, T_], F32))
            sqi32 = e0(nc.sbuf_tensor("sqi32", [32, T_], I32))
            q32 = e0(nc.sbuf_tensor("q32", [128, 32], F32))
            qi32 = e0(nc.sbuf_tensor("qi32", [128, 32], I32))
            q32b = e0(nc.sbuf_tensor("q32b", [128, 32], F32))
            t1 = e0(nc.sbuf_tensor("t1c", [128, 32], F32))
            t2 = e0(nc.sbuf_tensor("t2c", [128, 32], F32))

            ld = {}
            for nm, dst, src in [
                ("ident", identb[:], identf), ("tri", tri8b[:], tri8),
                ("pad", padb[:], padbin), ("b1", b1c[:], b1pt),
                ("th", thc[:], thpt), ("wr", wrc[:], wrpt), ("wi", wic[:], wipt),
                ("jc", jc[:], jcol), ("trq", trqT[:], trowq), ("trf", trfT[:], trowf),
                ("b2", b2rep[:], b2row.partition_broadcast(128)),
            ]:
                ld[nm] = P.emit("g", P.dma("g", dst, src), kind="dma")
            s_eps = P.emit("g", lambda g: g.memset(epsc[:], 1e-5))
            chk0 = P.gchk(list(ld.values()) + [s_eps])
            ld = {k: chk0 for k in ld}
            s_eps = chk0

            s_inv = P.emit("s", lambda s: s.activation(
                invc[:], jc[:], AF.Exp, scale=float(-np.log(10000.0))),
                deps=[ld["jc"]])

            def rope_scale(tr, n, outt, mul, deps):
                dfr = P.emit("v", lambda v: v.tensor_scalar(
                    sq32[:, 0:n], tr, invc[:], None, ALU.mult),
                    deps=list(deps) + [s_inv])
                dz = P.emit("v", lambda v: v.tensor_scalar(
                    sq32[:, 0:n], sq32[:, 0:n], 1.0 / TWO_PI, 0.125, ALU.mult, ALU.add),
                    deps=[dfr])
                di = P.emit("v", lambda v: v.tensor_copy(sqi32[:, 0:n], sq32[:, 0:n]),
                            deps=[dz])
                dd = P.emit("v", lambda v: v.scalar_tensor_tensor(
                    sq32[:, 0:n], sqi32[:, 0:n], -1.0, sq32[:, 0:n], ALU.mult, ALU.add),
                    deps=[di])
                ds = P.emit("s", lambda s: s.activation(
                    sq32[:, 0:n], sq32[:, 0:n], AF.Sin, scale=TWO_PI), deps=[dd])
                dm = P.emit("v", lambda v: v.tensor_scalar(
                    outt, sq32[:, 0:n], float(SQ2 * mul), None, ALU.mult), deps=[ds])
                return dm, ds

            dscf, dsf_s = rope_scale(trfT[:], T_, scfull[:], 1.0, [ld["trf"]])
            dscq, dsq_s = rope_scale(trqT[:], NQ, scqk[:], 1.0, [ld["trq"], dsf_s])
            dreps = []
            for m in range(4):
                dreps.append(P.emit("v", lambda v, m=m: v.tensor_scalar(
                    scqrep[32 * m:32 * m + 32, :], scqk[:], INV_SQ96, None, ALU.mult),
                    deps=[dscq]))

            dth = P.emit("v", lambda v: v.tensor_scalar(
                th2pi[:], thc[:], 1.0 / TWO_PI, None, ALU.mult), deps=[ld["th"]])
            dthb = P.emit("v", lambda v: v.scalar_tensor_tensor(
                thb1[:], b1c[:], 0.1 / TWO_PI, th2pi[:], ALU.mult, ALU.add),
                deps=[dth, ld["b1"]])
            dsb = P.emit("v", lambda v: v.tensor_scalar(
                sinb[:], b1c[:], 0.1, None, ALU.mult), deps=[ld["b1"]])
            s_cosb = P.emit("v", lambda v: v.tensor_scalar(
                cosb[:], b1c[:], 0.1, float(np.pi / 2), ALU.mult, ALU.add),
                deps=[ld["b1"], dsb])

            def redsin(zadd, outt, deps):
                dz = P.emit("v", lambda v: v.tensor_scalar(
                    q32[:], th2pi[:], 1.0, zadd, ALU.mult, ALU.add), deps=deps)
                di = P.emit("v", lambda v: v.tensor_copy(qi32[:], q32[:]), deps=[dz])
                dd = P.emit("v", lambda v: v.scalar_tensor_tensor(
                    q32b[:], qi32[:], -1.0, q32[:], ALU.mult, ALU.add), deps=[di])
                ds = P.emit("s", lambda s: s.activation(
                    outt, q32b[:], AF.Sin, scale=TWO_PI), deps=[dd])
                return ds

            dsin = redsin(0.0, t1[:], [dth])
            dcos = redsin(0.25, t2[:], [dth, dsin])
            da = P.emit("v", lambda v: v.tensor_tensor(t2[:], t2[:], wrc[:], ALU.mult),
                        deps=[dcos, ld["wr"]])
            db = P.emit("v", lambda v: v.tensor_tensor(t1[:], t1[:], wic[:], ALU.mult),
                        deps=[dsin, ld["wi"]])
            dc2 = P.emit("v", lambda v: v.tensor_tensor(t1[:], t1[:], t2[:], ALU.add),
                         deps=[da, db])
            dc3 = P.emit("v", lambda v: v.tensor_tensor(t1[:], t1[:], wic[:], ALU.mult),
                         deps=[dc2])
            dkq = P.emit("v", lambda v: v.tensor_scalar(
                kq[:], t1[:], 0.1, None, ALU.mult), deps=[dc3])
            dsin2 = redsin(0.0, t1[:], [dth, dkq])
            dcos2 = redsin(0.25, t2[:], [dth, dsin2])
            dks = P.emit("v", lambda v: v.tensor_tensor(ksin[:], kq[:], t1[:], ALU.mult),
                         deps=[dsin2, dkq])
            dkc = P.emit("v", lambda v: v.tensor_tensor(kcos[:], kq[:], t2[:], ALU.mult),
                         deps=[dcos2, dks])
            dkq = dkc
            s_ident = ld["ident"]
            s_tri = ld["tri"]
            s_pad = ld["pad"]
            s_b2 = ld["b2"]
            P.flush(blk)
        s_consts = dkq
        if upto < 1:
            raise _StopBuild()

        with ExitStack() as cA, nc.Block("p0ba") as blk:
            eA = cA.enter_context
            wdqb = [eA(nc.sbuf_tensor(f"wdqb{i}", [128, 8, 256], BF16)) for i in range(2)]
            tstg = [eA(nc.sbuf_tensor(f"tstg{i}", [128, 2048], BF16)) for i in range(2)]
            ptrp = eA(nc.psum_tensor("ptrp", [128, 2048], BF16))
            last_store = [None, None]
            last_use = [None, None]
            prev_ev = None
            st = None
            for kp in range(16):
                bi = kp % 2
                dl = P.emit("g", P.dma("g", wdqb[bi][:],
                    W_DQ[:, 256 * kp:256 * kp + 256].rearrange("(f p) k -> p f k", p=128)),
                    deps=[last_use[bi]], kind="dma")
                dl = P.gchk([dl])
                trs = []
                for a in range(2):
                    for i in range(8):
                        trs.append(P.emit("t", lambda t, a=a, i=i, bi=bi: t.transpose(
                            ptrp[:, a * 1024 + i * 128: a * 1024 + i * 128 + 128],
                            wdqb[bi][:, i, a * 128:a * 128 + 128], identb[:]),
                            deps=[dl, s_ident] + ([prev_ev] if (a == 0 and i == 0) else [])))
                last_use[bi] = trs[-1]
                ev = P.emit("v", lambda v, bi=bi: v.tensor_copy(tstg[bi][:], ptrp[:]),
                            deps=[trs[-1], last_store[bi]])
                prev_ev = ev
                st = P.emit("s", P.dma("s",
                    wdqT_s[256 * kp:256 * kp + 256, :].rearrange("(a p) i -> p a i", p=128),
                    tstg[bi][:].rearrange("p (a i) -> p a i", a=2)),
                    deps=[ev], kind="dma")
                last_store[bi] = st
            wdqT_done = st
            P.flush(blk)
        if upto < 2:
            raise _StopBuild()

        with ExitStack() as cB, nc.Block("p0bb") as blk:
            eB = cB.enter_context
            stg = [eB(nc.sbuf_tensor(f"wustg{i}", [128, 8, 1024], BF16)) for i in range(2)]
            last_st = [None, None]

            pend_stage = []

            def stage(srcap, dstap, bi, cols=1024):
                dlw = P.emit("g", P.dma("g", stg[bi][:, :, 0:cols], srcap),
                             deps=[last_st[bi]], kind="dma")
                pend_stage.append((dlw, dstap, bi, cols))
                if len(pend_stage) == 2:
                    chk = P.gchk([pend_stage[0][0], pend_stage[1][0]])
                    for _, dap, bj, cj in pend_stage:
                        last_st[bj] = P.emit("s", P.dma("s", dap, stg[bj][:, :, 0:cj]),
                                             deps=[chk], kind="dma")
                    pend_stage.clear()
                return last_st[bi]

            for sl in range(4):
                stage(W_UQ[1024 * sl:1024 * sl + 1024, :].rearrange("(f p) n -> p f n", p=128),
                      wu_s[1024 * sl:1024 * sl + 1024, 0:1024].rearrange("(f p) n -> p f n", p=128),
                      sl % 2)
            for sl in range(4):
                stage(W_QR[1024 * sl:1024 * sl + 1024, :].rearrange("(f p) n -> p f n", p=128),
                      wu_s[1024 * sl:1024 * sl + 1024, 1024:1536].rearrange("(f p) n -> p f n", p=128),
                      sl % 2, cols=512)
            for sl in range(4):
                stage(W1[:, 1024 * sl:1024 * sl + 1024].rearrange("(f p) n -> p f n", p=128),
                      w1_s[:, 1024 * sl:1024 * sl + 1024].rearrange("(f p) n -> p f n", p=128),
                      sl % 2)
            for sl in range(4):
                stage(W2[1024 * sl:1024 * sl + 1024, :].rearrange("(f p) n -> p f n", p=128),
                      w2_s[1024 * sl:1024 * sl + 1024, :].rearrange("(f p) n -> p f n", p=128),
                      sl % 2)
            wu_done = last_st[0]
            w_stage_done = [last_st[0], last_st[1]]

            wdqTk = [eB(nc.sbuf_tensor(f"wdqTk{i}", [128, 1024], BF16)) for i in range(2)]
            wuk = [eB(nc.sbuf_tensor(f"wuk{i}", [128, 512], BF16)) for i in range(2)]
            pacc = [eB(nc.psum_tensor(f"pacc{i}", [128, 512], F32)) for i in range(8)]
            last_mm = [None, None]
            prev_evac = None
            for js in range(3):
                mm_last = None
                for k in range(NCH):
                    bi = k % 2
                    dA = P.emit("y", P.dma("y", wdqTk[bi][:], wdqT_s[128 * k:128 * k + 128, :]),
                                deps=[wdqT_done, last_mm[bi]], kind="dma")
                    dB = P.emit("y", P.dma("y", wuk[bi][:],
                        wu_s[128 * k:128 * k + 128, 512 * js:512 * js + 512]),
                        deps=w_stage_done + [last_mm[bi]], kind="dma")
                    mms = []
                    for i in range(8):
                        mms.append(P.emit("t", lambda t, i=i, k=k, bi=bi: t.matmul(
                            pacc[i][:], wdqTk[bi][:, 128 * i:128 * i + 128], wuk[bi][:],
                            start=(k == 0), stop=(k == NCH - 1)),
                            deps=[dA, dB] + ([prev_evac] if k == 0 else [])))
                    mm_last = mms[-1]
                    last_mm[bi] = mm_last
                for i in range(8):
                    prev_evac = P.emit("v", lambda v, i=i, js=js: v.tensor_copy(
                        weff[:, i, 512 * js:512 * js + 512], pacc[i][:]), deps=[mm_last])
            weff_done = prev_evac
            P.flush(blk)
        if upto < 3:
            raise _StopBuild()

        c1 = ExitStack()
        with nc.Block("p1") as blk:
            e1 = c1.enter_context
            wdkv = e1(nc.sbuf_tensor("wdkv", [128, 8, 256], BF16))
            wuk2 = e1(nc.sbuf_tensor("wuk2", [128, 2, 1024], BF16))
            wuv2 = e1(nc.sbuf_tensor("wuv2", [128, 2, 1024], BF16))
            wkr = e1(nc.sbuf_tensor("wkr", [128, 8, 32], BF16))
            ckv = e1(nc.sbuf_tensor("ckv", [128, 2, T_], BF16))
            ckvd = e1(nc.sbuf_tensor("ckvd", [128, 2, NQ], BF16))
            kRf = e1(nc.sbuf_tensor("kRf", [33, T_], BF16))
            kRq = e1(nc.sbuf_tensor("kRq", [33, NQ], BF16))
            xt = [e1(nc.sbuf_tensor(f"xt{i}", [128, C_], F32)) for i in range(2)]
            xn = [e1(nc.sbuf_tensor(f"xn{i}", [128, C_], BF16)) for i in range(2)]
            xnT = [e1(nc.sbuf_tensor(f"xnT{i}", [128, 8, 128], BF16)) for i in range(2)]
            scr = e1(nc.sbuf_tensor("scr1", [128, C_], F32))
            stat = e1(nc.sbuf_tensor("stat", [128, 6], F32))
            c1p = ExitStack()
            ptr1 = c1p.enter_context(nc.psum_tensor("ptr1", [128, 1024], BF16))
            pckv = c1p.enter_context(nc.psum_tensor("pckv", [128, 256], F32))
            pkr = c1p.enter_context(nc.psum_tensor("pkr", [32, 128], F32))

            dwk = {}
            dwk["dkv"] = P.emit("g", P.dma("g", wdkv[:],
                W_DKV.rearrange("(f p) k -> p f k", p=128)), kind="dma")
            dwk["uk"] = P.emit("g", P.dma("g", wuk2[:],
                W_UK.rearrange("(a p) n -> p a n", p=128)), kind="dma")
            dwk["uv"] = P.emit("g", P.dma("g", wuv2[:],
                W_UV.rearrange("(a p) n -> p a n", p=128)), kind="dma")
            dwk["kr"] = P.emit("g", P.dma("g", wkr[:], WKRpt), kind="dma")
            chk1 = P.gchk(list(dwk.values()))
            dwk = {k: chk1 for k in dwk}
            mone = P.emit("g", lambda g: g.memset(kRf[32:33, :], 1.0))
            mone2 = P.emit("g", lambda g: g.memset(kRq[32:33, :], 1.0))

            free_x = [None, None]
            free_xn = [None, None]
            free_xnT = [None, None]
            prev_ck = None
            prev_kr = None
            prev_tr = None
            prev_stat = None
            prev_var = None
            xnq_last = None
            for it in range(48):
                bi = it % 2
                is_q = it >= 32
                pos = it - 32
                src = (xq[128 * pos:128 * pos + 128, :] if is_q
                       else xfull[128 * it:128 * it + 128, :])
                dx = P.emit("y", P.dma("y", xt[bi][:], src),
                            deps=[free_x[bi]], kind="dma")
                P.emit("v", lambda v: v.memzero(stat[:, 0:2]), deps=[prev_stat])
                dsum = P.emit("v", lambda v, bi=bi: v.reduce_sum(
                    stat[:, 0:1], xt[bi][:], axis=AX.X), deps=[dx, prev_stat])
                dsq0 = P.emit("s", lambda s, bi=bi: s.activation(
                    scr[:], xt[bi][:], AF.Square), deps=[dx, prev_var])
                dsq = P.emit("v", lambda v: v.reduce_sum(
                    stat[:, 1:2], scr[:], axis=AX.X), deps=[dsq0])
                dmean = P.emit("v", lambda v: v.tensor_scalar(
                    stat[:, 2:3], stat[:, 0:1], 1.0 / 1024.0, None, ALU.mult),
                    deps=[dsum])
                dvar = P.emit("v", lambda v: v.scalar_tensor_tensor(
                    stat[:, 3:4], stat[:, 2:3], -1.0, stat[:, 0:1], ALU.mult, ALU.mult),
                    deps=[dmean, dsq])
                dvar2 = P.emit("v", lambda v: v.tensor_tensor(
                    stat[:, 3:4], stat[:, 1:2], stat[:, 3:4], ALU.add), deps=[dvar])
                prev_var = dvar2
                dsd = P.emit("s", lambda s: s.activation(
                    stat[:, 4:5], stat[:, 3:4], AF.Sqrt, scale=1.0 / 1024.0,
                    bias=epsc[:]), deps=[dvar2, s_eps])
                drs = P.emit("v", lambda v: v.reciprocal(stat[:, 5:6], stat[:, 4:5]),
                             deps=[dsd])
                dap = P.emit("v", lambda v, bi=bi: v.tensor_scalar(
                    xn[bi][:], xt[bi][:], stat[:, 2:3], stat[:, 5:6],
                    ALU.subtract, ALU.mult), deps=[drs, free_xn[bi]])
                prev_stat = dap
                free_x[bi] = dap
                trs = []
                for f in range(8):
                    trs.append(P.emit("t", lambda t, f=f, bi=bi: t.transpose(
                        ptr1[:, 128 * f:128 * f + 128], xn[bi][:, 128 * f:128 * f + 128],
                        identb[:]), deps=[dap, s_ident] + ([prev_tr] if f == 0 else [])))
                dev = P.emit("v", lambda v, bi=bi: v.tensor_copy(
                    xnT[bi][:].rearrange("p f k -> p (f k)"), ptr1[:]),
                    deps=[trs[-1], free_xnT[bi]])
                prev_tr = dev
                free_xn[bi] = dev
                if is_q:
                    xnq_last = P.emit("v", lambda v, bi=bi, pos=pos: v.tensor_copy(
                        xnqT[:, :, 128 * pos:128 * pos + 128], xnT[bi][:]), deps=[dev])
                mm1 = []
                for k2 in range(2):
                    for f in range(8):
                        mm1.append(P.emit("t", lambda t, k2=k2, f=f, bi=bi: t.matmul(
                            pckv[:, 128 * k2:128 * k2 + 128],
                            wdkv[:, f, 128 * k2:128 * k2 + 128],
                            xnT[bi][:, f, :], start=(f == 0), stop=(f == 7)),
                            deps=[dev, dwk["dkv"]] + ([prev_ck] if (k2 == 0 and f == 0) else [])))
                tgt = ckvd if is_q else ckv
                col = 128 * (pos if is_q else it)
                prev_ck = P.emit("v", lambda v, tgt=tgt, col=col: v.tensor_copy(
                    tgt[:, :, col:col + 128], pckv[:].rearrange("p (a k) -> p a k", a=2)),
                    deps=[mm1[-1]])
                mm2 = []
                for f in range(8):
                    mm2.append(P.emit("t", lambda t, f=f, bi=bi: t.matmul(
                        pkr[:], wkr[:, f, :], xnT[bi][:, f, :],
                        start=(f == 0), stop=(f == 7)),
                        deps=[dev, dwk["kr"]] + ([prev_kr] if f == 0 else [])))
                free_xnT[bi] = mm2[-1]
                sc = scqk if is_q else scfull
                ktgt = kRq if is_q else kRf
                prev_kr = P.emit("v", lambda v, ktgt=ktgt, sc=sc, col=col: v.tensor_tensor(
                    ktgt[0:32, col:col + 128], pkr[:], sc[:, col:col + 128], ALU.mult),
                    deps=[mm2[-1], dscf, dscq])
            ln_kr_done = prev_kr
            ln_ck_done = prev_ck
            if "ckv" in taps:
                dbg_ckv = dt("dbg_ckv", [128, 2, T_], BF16, kind="ExternalOutput").ap()
                P.emit("g", P.dma("g", dbg_ckv, ckv[:]), deps=[ln_ck_done], kind="dma")
            if "ckvd" in taps:
                dbg_ckvd = dt("dbg_ckvd", [128, 2, NQ], BF16, kind="ExternalOutput").ap()
                P.emit("g", P.dma("g", dbg_ckvd, ckvd[:]), deps=[ln_ck_done], kind="dma")
            if "kRf" in taps:
                dbg_krf = dt("dbg_krf", [33, T_], BF16, kind="ExternalOutput").ap()
                P.emit("g", P.dma("g", dbg_krf, kRf[:]), deps=[ln_kr_done, mone], kind="dma")
            if "scf" in taps:
                dbg_scf = dt("dbg_scf", [32, T_], F32, kind="ExternalOutput").ap()
                P.emit("g", P.dma("g", dbg_scf, scfull[:]), deps=[dscf], kind="dma")
            if "xnqT" in taps:
                dbg_xnq = dt("dbg_xnq", [128, 8, NQ], BF16, kind="ExternalOutput").ap()
                P.emit("g", P.dma("g", dbg_xnq, xnqT[:]), deps=[xnq_last], kind="dma")
            if "weff" in taps:
                dbg_weff = dt("dbg_weff", [128, 8, 1536], BF16, kind="ExternalOutput").ap()
                P.emit("g", P.dma("g", dbg_weff, weff[:]), deps=[weff_done], kind="dma")
            P.flush(blk)
            c1p.close()
        if upto < 4:
            raise _StopBuild()

        with nc.Block("p1b") as blk:
            e1b = c1.enter_context
            vstg = [e1b(nc.sbuf_tensor(f"vstg{i}", [128, 16, 260], BF16)) for i in range(2)]
            kstg = [e1b(nc.sbuf_tensor(f"kstg{i}", [64, T_], BF16)) for i in range(2)]
            pv = [e1b(nc.psum_tensor(f"pv{i}", [128, 1024], F32)) for i in range(2)]
            pk = [e1b(nc.psum_tensor(f"pk{i}", [64, 512], F32)) for i in range(2)]
            vst_ones = []
            for i in range(2):
                vst_ones.append(P.emit("g", lambda g, i=i: g.memset(
                    vstg[i][:].rearrange("p h (c o) -> p h c o", o=65)[:, :, :, 64:65],
                    1.0)))

            prev_pv = [None, None]
            vb_store = [None, None]

            def vbuild(ckv_src, nchunks, dall, dep_ck):
                nonlocal prev_pv
                last_store = vb_store
                final = None
                for grp in range(nchunks // 4):
                    gb = grp % 2
                    ev_last = None
                    for tc4 in range(4):
                        tc = grp * 4 + tc4
                        pb = tc % 2
                        mms = []
                        for k2 in range(2):
                            for n2 in range(2):
                                mms.append(P.emit("t",
                                    lambda t, k2=k2, n2=n2, tc=tc, pb=pb, ckv_src=ckv_src: t.matmul(
                                        pv[pb][:, 512 * n2:512 * n2 + 512],
                                        ckv_src[:, k2, 128 * tc:128 * tc + 128],
                                        wuv2[:, k2, 512 * n2:512 * n2 + 512],
                                        start=(k2 == 0), stop=(k2 == 1)),
                                    deps=[dep_ck, dwk["uv"], prev_pv[pb]]))
                        ev_last = P.emit("v", lambda v, tc4=tc4, gb=gb, pb=pb: v.tensor_copy(
                            vstg[gb][:].rearrange("p h (c o) -> p h c o", o=65)[:, :, tc4, 0:64],
                            pv[pb][:].rearrange("p (h o) -> p h o", o=64)),
                            deps=[mms[-1], vst_ones[gb], last_store[gb]])
                        prev_pv[pb] = ev_last
                    for h in range(H_):
                        final = P.emit("s", P.dma("s",
                            dall[h, :, grp, :], vstg[gb][:, h, :]),
                            deps=[ev_last], kind="dma")
                    last_store[gb] = final
                return final

            v_done = vbuild(ckv, 32, vall, ln_ck_done)
            vd_done = vbuild(ckvd, 16, vdall, ln_ck_done)

            prev_pk = [None, None]
            kb_store = [None, None]

            def kbuild(ckv_src, ntok, dall, krsrc, dep_ck, dep_kr):
                nonlocal prev_pk
                prev_dma = kb_store
                final = None
                for h in range(H_):
                    hb = h % 2
                    ev_last = None
                    for ns in range(ntok // 512):
                        pb = ns % 2
                        mms = []
                        for k2 in range(2):
                            mms.append(P.emit("t",
                                lambda t, k2=k2, ns=ns, h=h, pb=pb, ckv_src=ckv_src: t.matmul(
                                    pk[pb][:],
                                    wuk2[:, k2, 64 * h:64 * h + 64],
                                    ckv_src[:, k2, 512 * ns:512 * ns + 512],
                                    start=(k2 == 0), stop=(k2 == 1)),
                                deps=[dep_ck, dwk["uk"], prev_pk[pb]]))
                        ev_last = P.emit("v", lambda v, ns=ns, hb=hb, pb=pb: v.tensor_copy(
                            kstg[hb][:, 512 * ns:512 * ns + 512], pk[pb][:]),
                            deps=[mms[-1], prev_dma[hb]])
                        prev_pk[pb] = ev_last
                    d1 = P.emit("s", P.dma("s", dall[h, 0:64, 0:ntok], kstg[hb][:, 0:ntok]),
                                deps=[ev_last], kind="dma")
                    d2 = P.emit("s", P.dma("s", dall[h, 64:97, 0:ntok], krsrc[:, 0:ntok]),
                                deps=list(dep_kr), kind="dma")
                    prev_dma[hb] = d1
                    final = d2
                return final

            k_done = kbuild(ckv, T_, kall, kRf, ln_ck_done, [ln_kr_done, mone])
            kd_done = kbuild(ckvd, NQ, kdall, kRq, ln_ck_done, [ln_kr_done, mone2])
            P.flush(blk)
        c1.close()
        kv_scratch_done = [v_done, vd_done, k_done, kd_done]
        if upto < 5:
            raise _StopBuild()

        qall_stores_all = []
        ffn_stores = []
        with ExitStack() as c2, nc.Block("p2") as blk:
            e2 = c2.enter_context
            qstg = e2(nc.sbuf_tensor("qstg", [97, 16, 512], BF16))
            actblk = e2(nc.sbuf_tensor("actblk", [128, 32, 512], BF16))
            w1b = [e2(nc.sbuf_tensor(f"w1b{i}", [128, 8, 256], BF16)) for i in range(2)]
            w2b = [e2(nc.sbuf_tensor(f"w2b{i}", [128, 2, 1024], BF16)) for i in range(2)]
            scrA_ = [e2(nc.sbuf_tensor(f"scrA{i}", [128, 512], F32)) for i in range(2)]
            scrB_ = [e2(nc.sbuf_tensor(f"scrB{i}", [128, 512], F32)) for i in range(2)]
            scrI = e2(nc.sbuf_tensor("scrI", [128, 512], I32))
            scrC = e2(nc.sbuf_tensor("scrC", [128, 512], F32))
            otile = [e2(nc.sbuf_tensor(f"otile{i}", [128, 1024], F32)) for i in range(2)]
            pq = [e2(nc.psum_tensor(f"pq{i}", [128, 512], F32)) for i in range(2)]
            pw2 = [e2(nc.psum_tensor(f"pw2{i}", [128, 1024], F32)) for i in range(2)]

            if "qchain" in taps:
                dbgq = [dt(f"dbgq{i}", [128, 512], F32, kind="ExternalOutput").ap() for i in range(5)]
                dbgq8 = dt("dbgq8", [128, 512], BF16, kind="ExternalOutput").ap()
            last_pq = [weff_done, weff_done]
            last_w1l = [None, None]
            last_w2l = [None, None]
            cur_w2l = [None, None]
            prev_u = [None, None]
            prev_u2 = [None, None]
            qall_prev = []
            ffn_st = [None, None]
            ffn_ev = [None, None]
            prev_d7 = [None, None]
            last_w2ev = [None, None]
            mq_prev = None
            for bl in range(4):
                cols = slice(512 * bl, 512 * bl + 512)
                mq96 = P.emit("g", lambda g: g.memset(qstg[96:97, :, :], -10.0),
                              deps=qall_prev + [mq_prev])
                mq_prev = mq96
                st16 = []
                for qc in range(12):
                    qb = qc % 2
                    mms = []
                    for f in range(8):
                        mms.append(P.emit("t", lambda t, f=f, qc=qc, qb=qb, cols=cols: t.matmul(
                            pq[qb][:], weff[:, f, 128 * qc:128 * qc + 128],
                            xnqT[:, f, cols], start=(f == 0), stop=(f == 7)),
                            deps=[weff_done, xnq_last, last_pq[qb]]))
                    if qc < 8:
                        P.emit("v", lambda v, qc=qc, qb=qb: v.tensor_scalar(
                            qstg[0:64, 2 * qc, :], pq[qb][0:64, :], INV_SQ96, None, ALU.mult),
                            deps=[mms[-1]] + qall_prev)
                        last_pq[qb] = P.emit("v", lambda v, qc=qc, qb=qb: v.tensor_scalar(
                            qstg[0:64, 2 * qc + 1, :], pq[qb][64:128, :], INV_SQ96, None,
                            ALU.mult), deps=[mms[-1]])
                    else:
                        evl = None
                        for m in range(4):
                            h = 4 * (qc - 8) + m
                            evl = P.emit("v", lambda v, m=m, h=h, qb=qb, cols=cols: v.tensor_tensor(
                                qstg[64:96, h, :], pq[qb][32 * m:32 * m + 32, :],
                                scqrep[32 * m:32 * m + 32, cols], ALU.mult),
                                deps=[mms[-1], dreps[3]] + qall_prev)
                        last_pq[qb] = evl
                for h in range(H_):
                    st16.append(P.emit("s", P.dma("s",
                        qall[h, :, cols], qstg[:, h, :]),
                        deps=[last_pq[0], last_pq[1], mq96], kind="dma"))
                qall_prev = st16
                qall_stores_all += st16

                if p2sub < 2:
                    continue
                for op in range(16):
                    wb = op % 2
                    dlw = P.emit("y", P.dma("y", w1b[wb][:],
                        w1_s[:, 256 * op:256 * op + 256].rearrange("(f p) k -> p f k", p=128)),
                        deps=[last_w1l[wb]] + (w_stage_done if bl == 0 and op < 2 else []),
                        kind="dma")
                    last_w1l[wb] = None
                    for o2 in range(2):
                        o = 2 * op + o2
                        hb = o % 2
                        mms = []
                        for f in range(8):
                            mms.append(P.emit("t", lambda t, f=f, o2=o2, wb=wb, hb=hb, cols=cols: t.matmul(
                                pq[hb][:], w1b[wb][:, f, 128 * o2:128 * o2 + 128],
                                xnqT[:, f, cols], start=(f == 0), stop=(f == 7)),
                                deps=[dlw, xnq_last, last_pq[hb]]))
                        if p2sub == 21:
                            d7 = P.emit("s", lambda s, o=o, hb=hb: s.activation(
                                actblk[:, o, :], pq[hb][:], AF.Relu, bias=b1c[:, o:o + 1]),
                                deps=[mms[-1]])
                            last_pq[hb] = d7
                            prev_d7 = d7
                            last_w1l[wb] = mms[-1]
                            continue
                        sA, sB = scrA_[hb], scrB_[hb]
                        dc_ = P.emit("s", lambda s, o=o, hb=hb, sA=sA: s.activation(
                            sA[:], pq[hb][:], AF.Sin, scale=0.1,
                            bias=cosb[:, o:o + 1]), deps=[mms[-1], s_cosb, prev_d7[hb]])
                        ds_ = P.emit("s", lambda s, o=o, hb=hb, sB=sB: s.activation(
                            sB[:], pq[hb][:], AF.Sin, scale=0.1,
                            bias=sinb[:, o:o + 1]), deps=[mms[-1], prev_u2[hb]])
                        du = P.emit("v", lambda v, o=o, sA=sA: v.tensor_scalar(
                            sA[:], sA[:], ksin[:, o:o + 1], None, ALU.mult),
                            deps=[dc_])
                        du2 = P.emit("v", lambda v, o=o, sA=sA, sB=sB: v.scalar_tensor_tensor(
                            sB[:], sB[:], kcos[:, o:o + 1], sA[:],
                            ALU.mult, ALU.add), deps=[ds_, du])
                        prev_u[hb] = du2
                        d6 = P.emit("v", lambda v, hb=hb, sA=sA, sB=sB: v.tensor_tensor(
                            sA[:], sB[:], pq[hb][:], ALU.add), deps=[du2])
                        prev_u2[hb] = d6
                        last_pq[hb] = d6
                        d7 = P.emit("s", lambda s, o=o, sA=sA: s.activation(
                            actblk[:, o, :], sA[:], AF.Relu, bias=b1c[:, o:o + 1]),
                            deps=[d6] + ([last_w2ev[0], last_w2ev[1]] if op == 0 and o2 == 0 else []))
                        prev_d7[hb] = d7
                        last_w1l[wb] = mms[-1]
                        if o % 2 == 0:
                            dl2 = P.emit("y", P.dma("y", w2b[wb][:],
                                w2_s[256 * op:256 * op + 256, :].rearrange("(a p) n -> p a n", p=128)),
                                deps=[last_w2l[wb]]
                                + (w_stage_done if bl == 0 and op < 2 else []), kind="dma")
                            cur_w2l[wb] = dl2
                        for ti in range(2):
                            for n2 in range(2):
                                mmf = P.emit("t",
                                    lambda t, o=o, o2=o2, ti=ti, n2=n2, wb=wb: t.matmul(
                                        pw2[ti][:, 512 * n2:512 * n2 + 512],
                                        actblk[:, o, 128 * ti:128 * ti + 128],
                                        w2b[wb][:, o2, 512 * n2:512 * n2 + 512],
                                        start=(o == 0), stop=(o == 31)),
                                    deps=[cur_w2l[wb], d7]
                                    + ([last_w2ev[ti]] if o == 0 else []))
                        last_w2l[wb] = mmf
                w1_done = P.emit("v", lambda v: v.nop(), deps=[prev_d7[0], prev_d7[1]])
                if bl == 0 and "act0" in taps:
                    dbg_act0 = dt("dbg_act0", [128, 32, 512], BF16, kind="ExternalOutput").ap()
                    P.emit("g", P.dma("g", dbg_act0, actblk[:]), deps=[w1_done], kind="dma")

                if p2sub < 3:
                    continue
                for ti in range(2):
                    ts4 = ti
                    ob = ts4 % 2
                    dev = P.emit("v", lambda v, ti=ti, ob=ob: v.tensor_tensor(
                        otile[ob][:], pw2[ti][:], b2rep[:], ALU.add),
                        deps=[last_w2l[0], last_w2l[1], s_b2, ffn_st[ob]])
                    last_w2ev[ti] = dev
                    ffn_st[ob] = P.emit("y", P.dma("y",
                        ffn_s[512 * bl + 128 * ts4:512 * bl + 128 * ts4 + 128, :],
                        otile[ob][:]), deps=[dev], kind="dma")
                    ffn_stores.append(ffn_st[ob])
                for tp in range(1, 2):
                    mmw = None
                    for op in range(16):
                        wb = op % 2
                        dlw = P.emit("y", P.dma("y", w2b[wb][:],
                            w2_s[256 * op:256 * op + 256, :].rearrange("(a p) n -> p a n", p=128)),
                            deps=[last_w2l[wb]] + (w_stage_done if bl == 0 and tp == 0 and op < 2 else []),
                            kind="dma")
                        for o2 in range(2):
                            o = 2 * op + o2
                            for ti in range(2):
                                ts4 = 2 * tp + ti
                                for n2 in range(2):
                                    mmw = P.emit("t",
                                        lambda t, o=o, o2=o2, ts4=ts4, ti=ti, n2=n2, wb=wb: t.matmul(
                                            pw2[ti][:, 512 * n2:512 * n2 + 512],
                                            actblk[:, o, 128 * ts4:128 * ts4 + 128],
                                            w2b[wb][:, o2, 512 * n2:512 * n2 + 512],
                                            start=(o == 0), stop=(o == 31)),
                                        deps=[dlw, w1_done]
                                        + ([last_w2ev[ti]] if o == 0 else []))
                        last_w2l[wb] = mmw
                    for ti in range(2):
                        ts4 = 2 * tp + ti
                        ob = ts4 % 2
                        dev = P.emit("v", lambda v, ti=ti, ob=ob: v.tensor_tensor(
                            otile[ob][:], pw2[ti][:], b2rep[:], ALU.add),
                            deps=[mmw, s_b2, ffn_st[ob]])
                        last_w2ev[ti] = dev
                        ffn_st[ob] = P.emit("y", P.dma("y",
                            ffn_s[512 * bl + 128 * ts4:512 * bl + 128 * ts4 + 128, :],
                            otile[ob][:]), deps=[dev], kind="dma")
                        ffn_stores.append(ffn_st[ob])
            if "act" in taps:
                dbg_act = dt("dbg_act", [128, 32, 512], BF16, kind="ExternalOutput").ap()
                P.emit("g", P.dma("g", dbg_act, actblk[:]), deps=[prev_d7], kind="dma")
            P.flush(blk)
        ctx12.close()
        if upto < 6:
            raise _StopBuild()

        ctx34 = ExitStack()
        e34 = ctx34.enter_context
        ycache = [e34(nc.sbuf_tensor(f"yc{j}", [128, NQ], F32)) for j in range(8)]

        with ExitStack() as c3, nc.Block("p3") as blk:
            e3 = c3.enter_context
            kh = [e3(nc.sbuf_tensor(f"kh{i}", [97, T_], BF16)) for i in range(2)]
            kdh = [e3(nc.sbuf_tensor(f"kdh{i}", [97, NQ], BF16)) for i in range(2)]
            vh = [e3(nc.sbuf_tensor(f"vh{i}", [128, 2144], BF16)) for i in range(2)]
            vdh = [e3(nc.sbuf_tensor(f"vdh{i}", [128, 1104], BF16)) for i in range(2)]
            qph = [e3(nc.sbuf_tensor(f"qph{i}", [97, NQ], BF16)) for i in range(2)]
            ptb = [e3(nc.sbuf_tensor(f"ptb{i}", [128, 1024], BF16)) for i in range(2)]
            dtmp = [e3(nc.sbuf_tensor(f"dtmp{i}", [1, NQ], F32)) for i in range(2)]
            ztile = e3(nc.sbuf_tensor("ztile", [128, 128], BF16))
            psc = [e3(nc.psum_tensor(f"psc{i}", [128, 1024], F32)) for i in range(2)]
            psy = e3(nc.psum_tensor("psy", [128, NQ], F32))

            zm = P.emit("g", lambda g: g.memset(ztile[:], 0.0))
            zc = P.gchk([zm])
            warm = None
            for p_ in range(NPOS):
                warm = P.emit("t", lambda t, p_=p_: t.matmul(
                    psy[:, 128 * p_:128 * p_ + 128], ztile[:], ztile[:],
                    start=True, stop=True), deps=[zc])
            HEAD_ORDER = [0] + list(range(H_))
            NH = len(HEAD_ORDER)
            head_done = [None] * NH
            yevac = [None] * NH
            denev = [None] * NH
            denst = [None] * NH
            for hi, h in enumerate(HEAD_ORDER):
                hb = hi % 2
                depf = head_done[hi - 2] if hi >= 2 else None
                dep0 = (kv_scratch_done + qall_stores_all) if hi == 0 else []
                l1 = P.emit("y", P.dma("y", kh[hb][:], kall[h, :, :]),
                            deps=dep0 + [depf], kind="dma")
                l2 = P.emit("y", P.dma("y", kdh[hb][:], kdall[h, :, :]),
                            deps=[depf], kind="dma")
                l3 = P.emit("y", P.dma("y",
                                       vh[hb][:, 0:2080].rearrange("p (g c) -> p g c", c=260),
                                       vall[h]),
                            deps=[depf], kind="dma")
                l4 = P.emit("y", P.dma("y",
                                       vdh[hb][:, 0:1040].rearrange("p (g c) -> p g c", c=260),
                                       vdall[h]),
                            deps=[depf], kind="dma")
                l5 = P.emit("y", P.dma("y", qph[hb][:], qall[h, :, :]),
                            deps=[depf], kind="dma")

                if hi == 1 and "p3in" in taps:
                    dbg_kh = dt("dbg_kh", [97, T_], BF16, kind="ExternalOutput").ap()
                    dbg_qh = dt("dbg_qh", [97, NQ], BF16, kind="ExternalOutput").ap()
                    dbg_vh = dt("dbg_vh", [128, 8, 260], BF16, kind="ExternalOutput").ap()
                    t1_ = P.emit("s", P.dma("s", dbg_kh, kh[0][:]), deps=[l1], kind="dma")
                    t2_ = P.emit("s", P.dma("s", dbg_qh, qph[0][:]), deps=[l5], kind="dma")
                    t3_ = P.emit("s", P.dma("s", dbg_vh, vh[0][:]), deps=[l3], kind="dma")
                    l1 = P.emit("g", lambda g: g.nop(), deps=[t1_, t2_, t3_])
                    l2 = l3 = l4 = l5 = l1
                exp_s = [None] * NGRP
                av_s = [None] * NGRP

                def emit_scores(gi):
                    kind, c, p0, ln, pad_abs = GROUPS[gi]
                    gb = gi % 2
                    mms = []
                    dep_psc = exp_s[gi - 2] if gi >= 2 else None
                    if kind == 0:
                        lhs = kh[hb][:, 128 * c:128 * c + 128]
                        for q0 in range(0, ln, 4):
                            w = min(4, ln - q0)
                            mms.append(P.emit("t",
                                lambda t, lhs=lhs, q0=q0, w=w, p0=p0, gb=gb, hb=hb: t.matmul(
                                    psc[gb][:, 128 * q0:128 * q0 + 128 * w], lhs,
                                    qph[hb][:, 128 * (p0 + q0):128 * (p0 + q0) + 128 * w],
                                    start=True, stop=True),
                                deps=[l1, l5, dep_psc]))
                    else:
                        for sl in range(ln):
                            p = p0 + sl
                            lhs = kdh[hb][:, 128 * p:128 * p + 128]
                            mms.append(P.emit("t", lambda t, lhs=lhs, p=p, sl=sl, gb=gb, hb=hb: t.matmul(
                                psc[gb][:, 128 * sl:128 * sl + 128], lhs,
                                qph[hb][:, 128 * p:128 * p + 128],
                                start=(sl % 4 == 0), stop=True),
                                deps=[l2, l5, dep_psc]))
                    vdep = None
                    if kind == 1:
                        vdep = P.emit("v", lambda v, gb=gb: v.tensor_tensor(
                            psc[gb][:], psc[gb][:], tri8b[:], ALU.add),
                            deps=[mms[-1], s_tri])
                    elif pad_abs is not None:
                        sl_loc = pad_abs - p0
                        vdep = P.emit("v", lambda v, gb=gb, sl_loc=sl_loc, pad_abs=pad_abs: v.tensor_scalar(
                            psc[gb][:, 128 * sl_loc:128 * sl_loc + 128],
                            psc[gb][:, 128 * sl_loc:128 * sl_loc + 128],
                            padb[:, pad_abs:pad_abs + 1], None, ALU.add),
                            deps=[mms[-1], s_pad])
                    dep_pt = av_s[gi - 2] if gi >= 2 else None
                    exp_s[gi] = P.emit("s", lambda s, gb=gb, ln=ln: s.activation(
                        ptb[gb][:, 0:128 * ln], psc[gb][:, 0:128 * ln], AF.Exp),
                        deps=[mms[-1], vdep, dep_pt])

                def emit_attv(gi):
                    kind, c, p0, ln, pad_abs = GROUPS[gi]
                    gb = gi % 2
                    avs = []
                    if kind == 0:
                        off = 260 * (c // 4) + 65 * (c % 4)
                        vap = vh[hb][:, off:off + 128][:, 0:65]
                        for q0 in range(0, ln, 4):
                            w = min(4, ln - q0)
                            first = (c == 0)
                            avs.append(P.emit("t",
                                lambda t, vap=vap, q0=q0, w=w, p0=p0, gb=gb, first=first: t.matmul(
                                    psy[0:65, 128 * (p0 + q0):128 * (p0 + q0) + 128 * w], vap,
                                    ptb[gb][:, 128 * q0:128 * q0 + 128 * w],
                                    start=first, stop=False),
                                deps=[exp_s[gi], l3, warm]
                                + ([yevac[hi - 1], denev[hi - 1]] if (gi == 0 and hi >= 1) else [])))
                    else:
                        for sl in range(ln):
                            p = p0 + sl
                            off = 260 * (p // 4) + 65 * (p % 4)
                            vap = vdh[hb][:, off:off + 128][:, 0:65]
                            avs.append(P.emit("t",
                                lambda t, vap=vap, p=p, sl=sl, gb=gb: t.matmul(
                                    psy[0:65, 128 * p:128 * p + 128], vap,
                                    ptb[gb][:, 128 * sl:128 * sl + 128],
                                    start=False, stop=True),
                                deps=[exp_s[gi], l4, warm]))
                    av_s[gi] = avs[-1]

                for gi in range(NGRP):
                    emit_scores(gi)
                    if gi >= 1:
                        emit_attv(gi - 1)
                emit_attv(NGRP - 1)
                head_done[hi] = av_s[-1]
                yevac[hi] = P.emit("v", lambda v, h=h: v.tensor_copy(
                    ycache[h // 2][64 * (h % 2):64 * (h % 2) + 64, :], psy[0:64, :]),
                    deps=[av_s[-1]])
                denev[hi] = P.emit("v", lambda v, h=h, hb=hb: v.tensor_copy(
                    dtmp[hb][:], psy[64:65, :]),
                    deps=[av_s[-1], denst[hi - 2] if hi >= 2 else None])
                denst[hi] = P.emit("s", P.dma("s", denh_s[h:h + 1, :], dtmp[hb][:]),
                                  deps=[denev[hi]], kind="dma")
            if "yc" in taps:
                dbg_yc = dt("dbg_yc", [128, 8, NQ], F32, kind="ExternalOutput").ap()
                for j in range(8):
                    P.emit("s", P.dma("s", dbg_yc[:, j, :], ycache[j][:]),
                           deps=list(yevac), kind="dma")
            P.flush(blk)

        if upto < 7:
            raise _StopBuild()
        with ExitStack() as c4, nc.Block("p4") as blk:
            e4 = c4.enter_context
            ybf = e4(nc.sbuf_tensor("ybf", [128, 8, NQ], BF16))
            wob = e4(nc.sbuf_tensor("wob", [128, 8, C_], BF16))
            dn128 = e4(nc.sbuf_tensor("dn128", [128, 256], F32))
            rrep = [e4(nc.sbuf_tensor("rrep0", [128, NQ], F32))] * 2
            xrt = [e4(nc.sbuf_tensor(f"xrt{i}", [128, C_], F32)) for i in range(2)]
            frt = [e4(nc.sbuf_tensor(f"frt{i}", [128, C_], F32)) for i in range(2)]
            ott = [e4(nc.sbuf_tensor(f"ott{i}", [128, C_], F32)) for i in range(2)]
            pwo = [e4(nc.psum_tensor(f"pwo{i}", [128, 1024], F32)) for i in range(2)]

            dwo = P.emit("g", P.dma("g", wob[:],
                                    W_O.rearrange("(j p) n -> p j n", p=128)), kind="dma")
            dwo = P.gchk([dwo])
            dr1 = P.emit("y", P.dma("y", dn128[:],
                                    denh_s.rearrange("h (a n) -> h a n", n=256)),
                         deps=list(denst), kind="dma")
            dr2 = P.emit("v", lambda v: v.reciprocal(dn128[:], dn128[:]), deps=[dr1])
            dr3 = P.emit("y", P.dma("y", denr_s.rearrange("h (a n) -> h a n", n=256),
                                    dn128[:]), deps=[dr2], kind="dma")
            norm = [None] * H_
            repl = [None, None]
            for h in range(H_):
                rb = h % 2
                repl[rb] = P.emit("y", P.dma("y", rrep[rb][:],
                                             denr_s[h:h + 1, :].partition_broadcast(128)),
                                  deps=[dr3, norm[h - 1] if h >= 1 else None], kind="dma")
                norm[h] = P.emit("v", lambda v, h=h, rb=rb: v.tensor_tensor(
                    ybf[64 * (h % 2):64 * (h % 2) + 64, h // 2, :],
                    ycache[h // 2][64 * (h % 2):64 * (h % 2) + 64, :],
                    rrep[rb][64 * (h % 2):64 * (h % 2) + 64, :], ALU.mult),
                    deps=[repl[rb], yevac[h + 1]])

            ost = [None, None]
            oev = [None, None]
            for pos in range(NPOS):
                pb = pos % 2
                dxr = P.emit("s", P.dma("s", xrt[pb][:], xq[128 * pos:128 * pos + 128, :]),
                             deps=[oev[pb]], kind="dma")
                dfr = P.emit("s", P.dma("s", frt[pb][:], ffn_s[128 * pos:128 * pos + 128, :]),
                             deps=[oev[pb]] + (ffn_stores if pos == 0 else []), kind="dma")
                mms = []
                for j in range(8):
                    for n2 in range(2):
                        mms.append(P.emit("t", lambda t, j=j, n2=n2, pos=pos, pb=pb: t.matmul(
                            pwo[pb][:, 512 * n2:512 * n2 + 512],
                            ybf[:, j, 128 * pos:128 * pos + 128],
                            wob[:, j, 512 * n2:512 * n2 + 512],
                            start=(j == 0), stop=(j == 7)),
                            deps=[dwo, norm[2 * j], norm[2 * j + 1]]
                            + ([oev[pb]] if j == 0 else [])))
                d1 = P.emit("v", lambda v, pb=pb: v.tensor_tensor(
                    ott[pb][:], pwo[pb][:], xrt[pb][:], ALU.add),
                    deps=[mms[-1], dxr, ost[pb]])
                d2 = P.emit("v", lambda v, pb=pb: v.tensor_tensor(
                    ott[pb][:], ott[pb][:], frt[pb][:], ALU.add), deps=[d1, dfr])
                oev[pb] = d2
                ost[pb] = P.emit("y", P.dma("y", out[128 * pos:128 * pos + 128, :],
                                            ott[pb][:]), deps=[d2], kind="dma")
            P.flush(blk)
        ctx34.close()

      _ = None
    except _StopBuild:
        pass
    return nc


_NC_CACHE = None


def _get_nc():
    global _NC_CACHE
    if _NC_CACHE is None:
        _NC_CACHE = build_nc()
    return _NC_CACHE


def _host_consts():
    c = {}
    c["identf"] = np.eye(128, dtype=np.float32)
    r = np.arange(128)
    tri = np.where(r[None, :] >= r[:, None], 0.0, NEG_BIG).astype(np.float32)
    c["tri8"] = np.ascontiguousarray(np.tile(tri, (1, 8)).astype(np.float32))
    c["trowf"] = np.ascontiguousarray(np.tile(np.arange(T_, dtype=np.float32), (32, 1)))
    c["jcol"] = ((np.arange(32) % 16) / 16.0).astype(np.float32).reshape(32, 1)
    return c


def kernel(L0_attn_x, L0_attn_W_DQ, L0_attn_W_UQ, L0_attn_W_DKV, L0_attn_W_UK,
           L0_attn_W_UV, L0_attn_W_QR, L0_attn_W_KR, L0_attn_W_O,
           L0_ffn_W1, L0_ffn_b1, L0_ffn_quantum_phases,
           L0_ffn_quantum_weights_real, L0_ffn_quantum_weights_imag,
           L0_ffn_W2, L0_ffn_b2):
    x = np.ascontiguousarray(np.asarray(L0_attn_x, dtype=np.float32))
    consts = _host_consts()
    shared = dict(
        W_DQ=np.ascontiguousarray(L0_attn_W_DQ, dtype=np.float32),
        W_UQ=np.ascontiguousarray(L0_attn_W_UQ, dtype=np.float32),
        W_QR=np.ascontiguousarray(L0_attn_W_QR, dtype=np.float32),
        W_DKV=np.ascontiguousarray(L0_attn_W_DKV, dtype=np.float32),
        W_UK=np.ascontiguousarray(L0_attn_W_UK, dtype=np.float32),
        W_UV=np.ascontiguousarray(L0_attn_W_UV, dtype=np.float32),
        WKRpt=np.ascontiguousarray(
            np.asarray(L0_attn_W_KR, dtype=np.float32).reshape(8, 128, 32).transpose(1, 0, 2)),
        W_O=np.ascontiguousarray(L0_attn_W_O, dtype=np.float32),
        W1=np.ascontiguousarray(L0_ffn_W1, dtype=np.float32),
        W2=np.ascontiguousarray(L0_ffn_W2, dtype=np.float32),
        b1pt=np.ascontiguousarray(np.asarray(L0_ffn_b1, dtype=np.float32).reshape(32, 128).T),
        thpt=np.ascontiguousarray(
            np.asarray(L0_ffn_quantum_phases, dtype=np.float32).reshape(32, 128).T),
        wrpt=np.ascontiguousarray(
            np.asarray(L0_ffn_quantum_weights_real, dtype=np.float32).reshape(32, 128).T),
        wipt=np.ascontiguousarray(
            np.asarray(L0_ffn_quantum_weights_imag, dtype=np.float32).reshape(32, 128).T),
        b2row=np.ascontiguousarray(np.asarray(L0_ffn_b2, dtype=np.float32).reshape(1, C_)),
        **consts,
    )
    in_maps = []
    qrows_all = []
    for core in range(8):
        b = core // 2
        tiles = TILES_A if core % 2 == 0 else TILES_B
        qrows = np.concatenate([np.arange(128 * j, 128 * j + 128) for j in tiles])
        qrows_all.append(qrows)
        padv = np.array([0.0 if tiles[p] == C_P[p] - 1 else NEG_BIG
                         for p in range(NPOS)], dtype=np.float32)
        im = dict(shared)
        im["xfull"] = np.ascontiguousarray(x[b])
        im["xq"] = np.ascontiguousarray(x[b][qrows])
        im["padbin"] = np.ascontiguousarray(np.tile(padv[None, :], (128, 1)).astype(np.float32))
        im["trowq"] = np.ascontiguousarray(np.tile(qrows.astype(np.float32)[None, :], (32, 1)))
        in_maps.append(im)
    nc = _get_nc()
    res = run_bass_kernel_spmd(nc, in_maps, core_ids=list(range(8)))
    full = np.empty((B_, T_, C_), dtype=np.float32)
    for core in range(8):
        full[core // 2][qrows_all[core]] = np.asarray(res.results[core]["out"])
    return full


# version 07 (final)
# speedup vs baseline: 1.1549x; 1.1549x over previous
import numpy as np
import concourse.bass as bass
import concourse.mybir as mybir
from concourse.bass_utils import run_bass_kernel_spmd
from contextlib import ExitStack

F32 = mybir.dt.float32
BF16 = mybir.dt.bfloat16
I32 = mybir.dt.int32
AF = mybir.ActivationFunctionType
ALU = mybir.AluOpType
AX = mybir.AxisListType

B_, T_, C_ = 4, 4096, 1024
H_ = 16
DFF_ = 4096
NQ = 2048
NPOS = 16
NCH = 32
C_P = [32 - 2 * p for p in range(NPOS)]
N_SLOTS = [sum(1 for p in range(NPOS) if C_P[p] - 2 >= c) for c in range(31)]
TWO_PI = float(2.0 * np.pi)
INV_SQ96 = float(1.0 / np.sqrt(96.0))
SQ2 = float(np.sqrt(2.0))
NEG_BIG = -30000.0

TILES_A = sorted([j for j in range(32) if j % 4 in (0, 3)], reverse=True)
TILES_B = sorted([j for j in range(32) if j % 4 in (1, 2)], reverse=True)
for _tiles in (TILES_A, TILES_B):
    for _p, _j in enumerate(_tiles):
        assert _j in (C_P[_p] - 1, C_P[_p] - 2) and C_P[_p] - 2 >= _j - 1
assert sorted(TILES_A + TILES_B) == list(range(32))

GROUPS = []
for _c in range(31):
    _n = N_SLOTS[_c]
    _p0 = 0
    while _p0 < _n:
        _ln = min(8, _n - _p0)
        _pad = (_n - 1) if (_c % 2 == 0 and _p0 <= _n - 1 < _p0 + _ln) else None
        GROUPS.append((0, _c, _p0, _ln, _pad))
        _p0 += _ln
GROUPS.append((1, -1, 0, 8, None))
GROUPS.append((1, -1, 8, 8, None))
NGRP = len(GROUPS)


class Prog:

    SECT = {"t": "tensor", "v": "vector", "s": "scalar", "y": "sync", "g": "gpsimd"}

    NDSEM = 48

    def __init__(self, nc, ctx):
        self.nc = nc
        self.sems = {e: ctx.enter_context(nc.semaphore("sem_" + e)) for e in self.SECT}
        self.cnt = dict.fromkeys(self.SECT, 0)
        self.pend = {e: [] for e in self.SECT}
        self.seen = {}
        self.steps = []
        self.dsems = [ctx.enter_context(nc.semaphore(f"dq{i}")) for i in range(self.NDSEM)]
        self.duse = [0] * self.NDSEM
        self.dnext = 0

    def emit(self, e, fn, deps=(), kind="op", incs=1):
        waits = []
        nd = False
        for d in deps:
            if d is None:
                continue
            pe, pv = self.steps[d]
            if pe == e:
                if kind == "op" and self.cnt[e] + 1 - pv <= 3:
                    nd = True
                continue
            key = (e, pe)
            if self.seen.get(key, 0) >= pv:
                continue
            self.seen[key] = pv
            waits.append(((self.sems[pe] if not isinstance(pe, tuple)
                           else self.dsems[pe[1]]), pv))
        sid = len(self.steps)
        if kind == "dma":
            i = self.dnext
            self.dnext = (self.dnext + 1) % self.NDSEM
            self.duse[i] += 1
            self.steps.append((("d", i), 16 * self.duse[i]))
            self.pend[e].append((waits, fn, kind, self.dsems[i], False))
        else:
            self.cnt[e] += 1
            self.steps.append((e, self.cnt[e]))
            self.pend[e].append((waits, fn, kind, None, nd))
        return sid

    def dma(self, e, out, in_):
        def fn(eng, sem, out=out, in_=in_):
            eng.dma_start(out, in_).then_inc(sem, 16)

        return fn

    def gchk(self, deps=()):
        def fn(g):
            g.drain(fusable=False)
            return g.nop()
        return self.emit("g", fn, deps=deps, kind="op")

    def drain(self, e):
        pairs = [(self.dsems[i], 16 * self.duse[i])
                 for i in range(self.NDSEM) if self.duse[i] > 0]

        def fn(eng, pairs=pairs):
            for sem, val in pairs:
                eng.wait_ge(sem, val)
            return eng.nop()

        return self.emit(e, fn, kind="op")

    def flush(self, block, drain=("y", "g", "s")):
        for e in drain:
            if self.pend[e]:
                if e == "g":
                    self.gchk()
                self.drain(e)
        for e in self.SECT:
            lst = self.pend[e]
            if not lst:
                continue
            sem = self.sems[e]

            def body(eng, lst=list(lst), sem=sem, e=e):
                for waits, fn, kind, dsem, nd in lst:
                    for (sh, v) in waits:
                        eng.wait_ge(sh, v)
                    if nd and e in ("v", "s"):
                        eng.drain(fusable=False)
                    if kind == "dma":
                        fn(eng, dsem)
                    else:
                        fn(eng).then_inc(sem, 1)

            getattr(block, self.SECT[e])(body)
            self.pend[e] = []


class _StopBuild(Exception):
    pass


def build_nc(upto=99, taps=(), p2sub=3):
    nc = bass.Bass("TRN2", target_bir_lowering=False)
    _dt0 = nc.dram_tensor

    def dt(name, shape, dtype, kind=None):
        if kind is None:
            kind = "ExternalOutput" if name in taps else "Internal"
        return _dt0(name, shape, dtype, kind=kind)

    xfull = dt("xfull", [T_, C_], F32, kind="ExternalInput").ap()
    xq = dt("xq", [NQ, C_], F32, kind="ExternalInput").ap()
    W_DQ = dt("W_DQ", [C_, DFF_], F32, kind="ExternalInput").ap()
    W_UQ = dt("W_UQ", [DFF_, C_], F32, kind="ExternalInput").ap()
    W_QR = dt("W_QR", [DFF_, 512], F32, kind="ExternalInput").ap()
    W_DKV = dt("W_DKV", [C_, 256], F32, kind="ExternalInput").ap()
    W_UK = dt("W_UK", [256, C_], F32, kind="ExternalInput").ap()
    W_UV = dt("W_UV", [256, C_], F32, kind="ExternalInput").ap()
    WKRpt = dt("WKRpt", [128, 8, 32], F32, kind="ExternalInput").ap()
    W_O = dt("W_O", [C_, C_], F32, kind="ExternalInput").ap()
    W1 = dt("W1", [C_, DFF_], F32, kind="ExternalInput").ap()
    W2 = dt("W2", [DFF_, C_], F32, kind="ExternalInput").ap()
    b1pt = dt("b1pt", [128, 32], F32, kind="ExternalInput").ap()
    thpt = dt("thpt", [128, 32], F32, kind="ExternalInput").ap()
    wrpt = dt("wrpt", [128, 32], F32, kind="ExternalInput").ap()
    wipt = dt("wipt", [128, 32], F32, kind="ExternalInput").ap()
    b2row = dt("b2row", [1, C_], F32, kind="ExternalInput").ap()
    identf = dt("identf", [128, 128], F32, kind="ExternalInput").ap()
    tri8 = dt("tri8", [128, 1024], F32, kind="ExternalInput").ap()
    padbin = dt("padbin", [128, 16], F32, kind="ExternalInput").ap()
    trowq = dt("trowq", [32, NQ], F32, kind="ExternalInput").ap()
    trowf = dt("trowf", [32, T_], F32, kind="ExternalInput").ap()
    jcol = dt("jcol", [32, 1], F32, kind="ExternalInput").ap()

    out = dt("out", [NQ, C_], F32, kind="ExternalOutput").ap()

    wdqT_s = dt("wdqT_s", [DFF_, C_], BF16).ap()
    wu_s = dt("wu_s", [DFF_, 1536], BF16).ap()
    w1_s = dt("w1_s", [C_, DFF_], BF16).ap()
    w2_s = dt("w2_s", [DFF_, C_], BF16).ap()
    kall = dt("kall", [H_, 97, T_], BF16).ap()
    kdall = dt("kdall", [H_, 97, NQ], BF16).ap()
    vall = dt("vall", [H_, 128, 8, 260], BF16).ap()
    vdall = dt("vdall", [H_, 128, 4, 260], BF16).ap()
    qall = dt("qall", [H_, 97, NQ], BF16).ap()
    ffn_s = dt("ffn_s", [NQ, C_], F32).ap()
    denr_s = dt("denr_s", [16, NQ], F32).ap()
    denh_s = dt("denh_s", [16, NQ], F32).ap()

    ctx = ExitStack()
    try:
      with ctx:
        e = ctx.enter_context
        P = Prog(nc, ctx)

        identb = e(nc.sbuf_tensor("identb", [128, 128], BF16))
        tri8b = e(nc.sbuf_tensor("tri8b", [128, 1024], F32))
        padb = e(nc.sbuf_tensor("padb", [128, 16], F32))
        b2rep = e(nc.sbuf_tensor("b2rep", [128, C_], F32))
        b1c = e(nc.sbuf_tensor("b1c", [128, 32], F32))
        th2pi = e(nc.sbuf_tensor("th2pi", [128, 32], F32))
        thb1 = e(nc.sbuf_tensor("thb1", [128, 32], F32))
        kq = e(nc.sbuf_tensor("kq", [128, 32], F32))
        ksin = e(nc.sbuf_tensor("ksin", [128, 32], F32))
        kcos = e(nc.sbuf_tensor("kcos", [128, 32], F32))
        sinb = e(nc.sbuf_tensor("sinb", [128, 32], F32))
        cosb = e(nc.sbuf_tensor("cosb", [128, 32], F32))
        scfull = e(nc.sbuf_tensor("scfull", [32, T_], F32))
        scqk = e(nc.sbuf_tensor("scqk", [32, NQ], F32))
        scqrep = e(nc.sbuf_tensor("scqrep", [128, NQ], F32))
        epsc = e(nc.sbuf_tensor("epsc", [128, 1], F32))

        ctx12 = ExitStack()
        e12 = ctx12.enter_context
        xnqT = e12(nc.sbuf_tensor("xnqT", [128, 8, NQ], BF16))
        weff = e12(nc.sbuf_tensor("weff", [128, 8, 1536], BF16))

        with ExitStack() as c0, nc.Block("p0") as blk:
            e0 = c0.enter_context
            thc = e0(nc.sbuf_tensor("thc", [128, 32], F32))
            wrc = e0(nc.sbuf_tensor("wrc", [128, 32], F32))
            wic = e0(nc.sbuf_tensor("wic", [128, 32], F32))
            jc = e0(nc.sbuf_tensor("jc", [32, 1], F32))
            invc = e0(nc.sbuf_tensor("invc", [32, 1], F32))
            trqT = e0(nc.sbuf_tensor("trqT", [32, NQ], F32))
            trfT = e0(nc.sbuf_tensor("trfT", [32, T_], F32))
            sq32 = e0(nc.sbuf_tensor("sq32", [32, T_], F32))
            sqi32 = e0(nc.sbuf_tensor("sqi32", [32, T_], I32))
            q32 = e0(nc.sbuf_tensor("q32", [128, 32], F32))
            qi32 = e0(nc.sbuf_tensor("qi32", [128, 32], I32))
            q32b = e0(nc.sbuf_tensor("q32b", [128, 32], F32))
            t1 = e0(nc.sbuf_tensor("t1c", [128, 32], F32))
            t2 = e0(nc.sbuf_tensor("t2c", [128, 32], F32))

            ld = {}
            for nm, dst, src in [
                ("ident", identb[:], identf), ("tri", tri8b[:], tri8),
                ("pad", padb[:], padbin), ("b1", b1c[:], b1pt),
                ("th", thc[:], thpt), ("wr", wrc[:], wrpt), ("wi", wic[:], wipt),
                ("jc", jc[:], jcol), ("trq", trqT[:], trowq), ("trf", trfT[:], trowf),
                ("b2", b2rep[:], b2row.partition_broadcast(128)),
            ]:
                ld[nm] = P.emit("g", P.dma("g", dst, src), kind="dma")
            s_eps = P.emit("g", lambda g: g.memset(epsc[:], 1e-5))
            chk0 = P.gchk(list(ld.values()) + [s_eps])
            ld = {k: chk0 for k in ld}
            s_eps = chk0

            s_inv = P.emit("s", lambda s: s.activation(
                invc[:], jc[:], AF.Exp, scale=float(-np.log(10000.0))),
                deps=[ld["jc"]])

            def rope_scale(tr, n, outt, mul, deps):
                dfr = P.emit("v", lambda v: v.tensor_scalar(
                    sq32[:, 0:n], tr, invc[:], None, ALU.mult),
                    deps=list(deps) + [s_inv])
                dz = P.emit("v", lambda v: v.tensor_scalar(
                    sq32[:, 0:n], sq32[:, 0:n], 1.0 / TWO_PI, 0.125, ALU.mult, ALU.add),
                    deps=[dfr])
                di = P.emit("v", lambda v: v.tensor_copy(sqi32[:, 0:n], sq32[:, 0:n]),
                            deps=[dz])
                dd = P.emit("v", lambda v: v.scalar_tensor_tensor(
                    sq32[:, 0:n], sqi32[:, 0:n], -1.0, sq32[:, 0:n], ALU.mult, ALU.add),
                    deps=[di])
                ds = P.emit("s", lambda s: s.activation(
                    sq32[:, 0:n], sq32[:, 0:n], AF.Sin, scale=TWO_PI), deps=[dd])
                dm = P.emit("v", lambda v: v.tensor_scalar(
                    outt, sq32[:, 0:n], float(SQ2 * mul), None, ALU.mult), deps=[ds])
                return dm, ds

            dscf, dsf_s = rope_scale(trfT[:], T_, scfull[:], 1.0, [ld["trf"]])
            dscq, dsq_s = rope_scale(trqT[:], NQ, scqk[:], 1.0, [ld["trq"], dsf_s])
            dreps = []
            for m in range(4):
                dreps.append(P.emit("v", lambda v, m=m: v.tensor_scalar(
                    scqrep[32 * m:32 * m + 32, :], scqk[:], INV_SQ96, None, ALU.mult),
                    deps=[dscq]))

            dth = P.emit("v", lambda v: v.tensor_scalar(
                th2pi[:], thc[:], 1.0 / TWO_PI, None, ALU.mult), deps=[ld["th"]])
            dthb = P.emit("v", lambda v: v.scalar_tensor_tensor(
                thb1[:], b1c[:], 0.1 / TWO_PI, th2pi[:], ALU.mult, ALU.add),
                deps=[dth, ld["b1"]])
            dsb = P.emit("v", lambda v: v.tensor_scalar(
                sinb[:], b1c[:], 0.1, None, ALU.mult), deps=[ld["b1"]])
            s_cosb = P.emit("v", lambda v: v.tensor_scalar(
                cosb[:], b1c[:], 0.1, float(np.pi / 2), ALU.mult, ALU.add),
                deps=[ld["b1"], dsb])

            def redsin(zadd, outt, deps):
                dz = P.emit("v", lambda v: v.tensor_scalar(
                    q32[:], th2pi[:], 1.0, zadd, ALU.mult, ALU.add), deps=deps)
                di = P.emit("v", lambda v: v.tensor_copy(qi32[:], q32[:]), deps=[dz])
                dd = P.emit("v", lambda v: v.scalar_tensor_tensor(
                    q32b[:], qi32[:], -1.0, q32[:], ALU.mult, ALU.add), deps=[di])
                ds = P.emit("s", lambda s: s.activation(
                    outt, q32b[:], AF.Sin, scale=TWO_PI), deps=[dd])
                return ds

            dsin = redsin(0.0, t1[:], [dth])
            dcos = redsin(0.25, t2[:], [dth, dsin])
            da = P.emit("v", lambda v: v.tensor_tensor(t2[:], t2[:], wrc[:], ALU.mult),
                        deps=[dcos, ld["wr"]])
            db = P.emit("v", lambda v: v.tensor_tensor(t1[:], t1[:], wic[:], ALU.mult),
                        deps=[dsin, ld["wi"]])
            dc2 = P.emit("v", lambda v: v.tensor_tensor(t1[:], t1[:], t2[:], ALU.add),
                         deps=[da, db])
            dc3 = P.emit("v", lambda v: v.tensor_tensor(t1[:], t1[:], wic[:], ALU.mult),
                         deps=[dc2])
            dkq = P.emit("v", lambda v: v.tensor_scalar(
                kq[:], t1[:], 0.1, None, ALU.mult), deps=[dc3])
            dsin2 = redsin(0.0, t1[:], [dth, dkq])
            dcos2 = redsin(0.25, t2[:], [dth, dsin2])
            dks = P.emit("v", lambda v: v.tensor_tensor(ksin[:], kq[:], t1[:], ALU.mult),
                         deps=[dsin2, dkq])
            dkc = P.emit("v", lambda v: v.tensor_tensor(kcos[:], kq[:], t2[:], ALU.mult),
                         deps=[dcos2, dks])
            dkq = dkc
            s_ident = ld["ident"]
            s_tri = ld["tri"]
            s_pad = ld["pad"]
            s_b2 = ld["b2"]
            P.flush(blk)
        s_consts = dkq
        if upto < 1:
            raise _StopBuild()

        with ExitStack() as cA, nc.Block("p0ba") as blk:
            eA = cA.enter_context
            wdqb = [eA(nc.sbuf_tensor(f"wdqb{i}", [128, 8, 256], BF16)) for i in range(2)]
            tstg = [eA(nc.sbuf_tensor(f"tstg{i}", [128, 2048], BF16)) for i in range(2)]
            ptrp = eA(nc.psum_tensor("ptrp", [128, 2048], BF16))
            last_store = [None, None]
            last_use = [None, None]
            prev_ev = None
            st = None
            for kp in range(16):
                bi = kp % 2
                dl = P.emit("g", P.dma("g", wdqb[bi][:],
                    W_DQ[:, 256 * kp:256 * kp + 256].rearrange("(f p) k -> p f k", p=128)),
                    deps=[last_use[bi]], kind="dma")
                dl = P.gchk([dl])
                trs = []
                for a in range(2):
                    for i in range(8):
                        trs.append(P.emit("t", lambda t, a=a, i=i, bi=bi: t.transpose(
                            ptrp[:, a * 1024 + i * 128: a * 1024 + i * 128 + 128],
                            wdqb[bi][:, i, a * 128:a * 128 + 128], identb[:]),
                            deps=[dl, s_ident] + ([prev_ev] if (a == 0 and i == 0) else [])))
                last_use[bi] = trs[-1]
                ev = P.emit("v", lambda v, bi=bi: v.tensor_copy(tstg[bi][:], ptrp[:]),
                            deps=[trs[-1], last_store[bi]])
                prev_ev = ev
                st = P.emit("s", P.dma("s",
                    wdqT_s[256 * kp:256 * kp + 256, :].rearrange("(a p) i -> p a i", p=128),
                    tstg[bi][:].rearrange("p (a i) -> p a i", a=2)),
                    deps=[ev], kind="dma")
                last_store[bi] = st
            wdqT_done = st
            P.flush(blk)
        if upto < 2:
            raise _StopBuild()

        with ExitStack() as cB, nc.Block("p0bb") as blk:
            eB = cB.enter_context
            stg = [eB(nc.sbuf_tensor(f"wustg{i}", [128, 8, 1024], BF16)) for i in range(2)]
            last_st = [None, None]

            pend_stage = []

            def stage(srcap, dstap, bi, cols=1024):
                dlw = P.emit("g", P.dma("g", stg[bi][:, :, 0:cols], srcap),
                             deps=[last_st[bi]], kind="dma")
                pend_stage.append((dlw, dstap, bi, cols))
                if len(pend_stage) == 2:
                    chk = P.gchk([pend_stage[0][0], pend_stage[1][0]])
                    for _, dap, bj, cj in pend_stage:
                        last_st[bj] = P.emit("s", P.dma("s", dap, stg[bj][:, :, 0:cj]),
                                             deps=[chk], kind="dma")
                    pend_stage.clear()
                return last_st[bi]

            for sl in range(4):
                stage(W_UQ[1024 * sl:1024 * sl + 1024, :].rearrange("(f p) n -> p f n", p=128),
                      wu_s[1024 * sl:1024 * sl + 1024, 0:1024].rearrange("(f p) n -> p f n", p=128),
                      sl % 2)
            for sl in range(4):
                stage(W_QR[1024 * sl:1024 * sl + 1024, :].rearrange("(f p) n -> p f n", p=128),
                      wu_s[1024 * sl:1024 * sl + 1024, 1024:1536].rearrange("(f p) n -> p f n", p=128),
                      sl % 2, cols=512)
            for sl in range(4):
                stage(W1[:, 1024 * sl:1024 * sl + 1024].rearrange("(f p) n -> p f n", p=128),
                      w1_s[:, 1024 * sl:1024 * sl + 1024].rearrange("(f p) n -> p f n", p=128),
                      sl % 2)
            for sl in range(4):
                stage(W2[1024 * sl:1024 * sl + 1024, :].rearrange("(f p) n -> p f n", p=128),
                      w2_s[1024 * sl:1024 * sl + 1024, :].rearrange("(f p) n -> p f n", p=128),
                      sl % 2)
            wu_done = last_st[0]
            w_stage_done = [last_st[0], last_st[1]]

            wdqTk = [eB(nc.sbuf_tensor(f"wdqTk{i}", [128, 1024], BF16)) for i in range(2)]
            wuk = [eB(nc.sbuf_tensor(f"wuk{i}", [128, 512], BF16)) for i in range(2)]
            pacc = [eB(nc.psum_tensor(f"pacc{i}", [128, 512], F32)) for i in range(8)]
            last_mm = [None, None]
            prev_evac = None
            for js in range(3):
                mm_last = None
                for k in range(NCH):
                    bi = k % 2
                    dA = P.emit("y", P.dma("y", wdqTk[bi][:], wdqT_s[128 * k:128 * k + 128, :]),
                                deps=[wdqT_done, last_mm[bi]], kind="dma")
                    dB = P.emit("y", P.dma("y", wuk[bi][:],
                        wu_s[128 * k:128 * k + 128, 512 * js:512 * js + 512]),
                        deps=w_stage_done + [last_mm[bi]], kind="dma")
                    mms = []
                    for i in range(8):
                        mms.append(P.emit("t", lambda t, i=i, k=k, bi=bi: t.matmul(
                            pacc[i][:], wdqTk[bi][:, 128 * i:128 * i + 128], wuk[bi][:],
                            start=(k == 0), stop=(k == NCH - 1)),
                            deps=[dA, dB] + ([prev_evac] if k == 0 else [])))
                    mm_last = mms[-1]
                    last_mm[bi] = mm_last
                for i in range(8):
                    prev_evac = P.emit("v", lambda v, i=i, js=js: v.tensor_copy(
                        weff[:, i, 512 * js:512 * js + 512], pacc[i][:]), deps=[mm_last])
            weff_done = prev_evac
            P.flush(blk)
        if upto < 3:
            raise _StopBuild()

        c1 = ExitStack()
        with nc.Block("p1") as blk:
            e1 = c1.enter_context
            wdkv = e1(nc.sbuf_tensor("wdkv", [128, 8, 256], BF16))
            wuk2 = e1(nc.sbuf_tensor("wuk2", [128, 2, 1024], BF16))
            wuv2 = e1(nc.sbuf_tensor("wuv2", [128, 2, 1024], BF16))
            wkr = e1(nc.sbuf_tensor("wkr", [128, 8, 32], BF16))
            ckv = e1(nc.sbuf_tensor("ckv", [128, 2, T_], BF16))
            ckvd = e1(nc.sbuf_tensor("ckvd", [128, 2, NQ], BF16))
            kRf = e1(nc.sbuf_tensor("kRf", [33, T_], BF16))
            kRq = e1(nc.sbuf_tensor("kRq", [33, NQ], BF16))
            xt = [e1(nc.sbuf_tensor(f"xt{i}", [128, C_], F32)) for i in range(2)]
            xn = [e1(nc.sbuf_tensor(f"xn{i}", [128, C_], BF16)) for i in range(2)]
            xnT = [e1(nc.sbuf_tensor(f"xnT{i}", [128, 8, 128], BF16)) for i in range(2)]
            scr = e1(nc.sbuf_tensor("scr1", [128, C_], F32))
            stat = e1(nc.sbuf_tensor("stat", [128, 6], F32))
            c1p = ExitStack()
            ptr1 = c1p.enter_context(nc.psum_tensor("ptr1", [128, 1024], BF16))
            pckv = c1p.enter_context(nc.psum_tensor("pckv", [128, 256], F32))
            pkr = c1p.enter_context(nc.psum_tensor("pkr", [32, 128], F32))

            dwk = {}
            dwk["dkv"] = P.emit("g", P.dma("g", wdkv[:],
                W_DKV.rearrange("(f p) k -> p f k", p=128)), kind="dma")
            dwk["uk"] = P.emit("g", P.dma("g", wuk2[:],
                W_UK.rearrange("(a p) n -> p a n", p=128)), kind="dma")
            dwk["uv"] = P.emit("g", P.dma("g", wuv2[:],
                W_UV.rearrange("(a p) n -> p a n", p=128)), kind="dma")
            dwk["kr"] = P.emit("g", P.dma("g", wkr[:], WKRpt), kind="dma")
            chk1 = P.gchk(list(dwk.values()))
            dwk = {k: chk1 for k in dwk}
            mone = P.emit("g", lambda g: g.memset(kRf[32:33, :], 1.0))
            mone2 = P.emit("g", lambda g: g.memset(kRq[32:33, :], 1.0))

            free_x = [None, None]
            free_xn = [None, None]
            free_xnT = [None, None]
            prev_ck = None
            prev_kr = None
            prev_tr = None
            prev_stat = None
            prev_var = None
            xnq_last = None
            for it in range(48):
                bi = it % 2
                is_q = it >= 32
                pos = it - 32
                src = (xq[128 * pos:128 * pos + 128, :] if is_q
                       else xfull[128 * it:128 * it + 128, :])
                dx = P.emit("y", P.dma("y", xt[bi][:], src),
                            deps=[free_x[bi]], kind="dma")
                P.emit("v", lambda v: v.memzero(stat[:, 0:2]), deps=[prev_stat])
                dsum = P.emit("v", lambda v, bi=bi: v.reduce_sum(
                    stat[:, 0:1], xt[bi][:], axis=AX.X), deps=[dx, prev_stat])
                dsq0 = P.emit("s", lambda s, bi=bi: s.activation(
                    scr[:], xt[bi][:], AF.Square), deps=[dx, prev_var])
                dsq = P.emit("v", lambda v: v.reduce_sum(
                    stat[:, 1:2], scr[:], axis=AX.X), deps=[dsq0])
                dmean = P.emit("v", lambda v: v.tensor_scalar(
                    stat[:, 2:3], stat[:, 0:1], 1.0 / 1024.0, None, ALU.mult),
                    deps=[dsum])
                dvar = P.emit("v", lambda v: v.scalar_tensor_tensor(
                    stat[:, 3:4], stat[:, 2:3], -1.0, stat[:, 0:1], ALU.mult, ALU.mult),
                    deps=[dmean, dsq])
                dvar2 = P.emit("v", lambda v: v.tensor_tensor(
                    stat[:, 3:4], stat[:, 1:2], stat[:, 3:4], ALU.add), deps=[dvar])
                prev_var = dvar2
                dsd = P.emit("s", lambda s: s.activation(
                    stat[:, 4:5], stat[:, 3:4], AF.Sqrt, scale=1.0 / 1024.0,
                    bias=epsc[:]), deps=[dvar2, s_eps])
                drs = P.emit("v", lambda v: v.reciprocal(stat[:, 5:6], stat[:, 4:5]),
                             deps=[dsd])
                dap = P.emit("v", lambda v, bi=bi: v.tensor_scalar(
                    xn[bi][:], xt[bi][:], stat[:, 2:3], stat[:, 5:6],
                    ALU.subtract, ALU.mult), deps=[drs, free_xn[bi]])
                prev_stat = dap
                free_x[bi] = dap
                trs = []
                for f in range(8):
                    trs.append(P.emit("t", lambda t, f=f, bi=bi: t.transpose(
                        ptr1[:, 128 * f:128 * f + 128], xn[bi][:, 128 * f:128 * f + 128],
                        identb[:]), deps=[dap, s_ident] + ([prev_tr] if f == 0 else [])))
                dev = P.emit("v", lambda v, bi=bi: v.tensor_copy(
                    xnT[bi][:].rearrange("p f k -> p (f k)"), ptr1[:]),
                    deps=[trs[-1], free_xnT[bi]])
                prev_tr = dev
                free_xn[bi] = dev
                if is_q:
                    xnq_last = P.emit("v", lambda v, bi=bi, pos=pos: v.tensor_copy(
                        xnqT[:, :, 128 * pos:128 * pos + 128], xnT[bi][:]), deps=[dev])
                mm1 = []
                for k2 in range(2):
                    for f in range(8):
                        mm1.append(P.emit("t", lambda t, k2=k2, f=f, bi=bi: t.matmul(
                            pckv[:, 128 * k2:128 * k2 + 128],
                            wdkv[:, f, 128 * k2:128 * k2 + 128],
                            xnT[bi][:, f, :], start=(f == 0), stop=(f == 7)),
                            deps=[dev, dwk["dkv"]] + ([prev_ck] if (k2 == 0 and f == 0) else [])))
                tgt = ckvd if is_q else ckv
                col = 128 * (pos if is_q else it)
                prev_ck = P.emit("v", lambda v, tgt=tgt, col=col: v.tensor_copy(
                    tgt[:, :, col:col + 128], pckv[:].rearrange("p (a k) -> p a k", a=2)),
                    deps=[mm1[-1]])
                mm2 = []
                for f in range(8):
                    mm2.append(P.emit("t", lambda t, f=f, bi=bi: t.matmul(
                        pkr[:], wkr[:, f, :], xnT[bi][:, f, :],
                        start=(f == 0), stop=(f == 7)),
                        deps=[dev, dwk["kr"]] + ([prev_kr] if f == 0 else [])))
                free_xnT[bi] = mm2[-1]
                sc = scqk if is_q else scfull
                ktgt = kRq if is_q else kRf
                prev_kr = P.emit("v", lambda v, ktgt=ktgt, sc=sc, col=col: v.tensor_tensor(
                    ktgt[0:32, col:col + 128], pkr[:], sc[:, col:col + 128], ALU.mult),
                    deps=[mm2[-1], dscf, dscq])
            ln_kr_done = prev_kr
            ln_ck_done = prev_ck
            if "ckv" in taps:
                dbg_ckv = dt("dbg_ckv", [128, 2, T_], BF16, kind="ExternalOutput").ap()
                P.emit("g", P.dma("g", dbg_ckv, ckv[:]), deps=[ln_ck_done], kind="dma")
            if "ckvd" in taps:
                dbg_ckvd = dt("dbg_ckvd", [128, 2, NQ], BF16, kind="ExternalOutput").ap()
                P.emit("g", P.dma("g", dbg_ckvd, ckvd[:]), deps=[ln_ck_done], kind="dma")
            if "kRf" in taps:
                dbg_krf = dt("dbg_krf", [33, T_], BF16, kind="ExternalOutput").ap()
                P.emit("g", P.dma("g", dbg_krf, kRf[:]), deps=[ln_kr_done, mone], kind="dma")
            if "scf" in taps:
                dbg_scf = dt("dbg_scf", [32, T_], F32, kind="ExternalOutput").ap()
                P.emit("g", P.dma("g", dbg_scf, scfull[:]), deps=[dscf], kind="dma")
            if "xnqT" in taps:
                dbg_xnq = dt("dbg_xnq", [128, 8, NQ], BF16, kind="ExternalOutput").ap()
                P.emit("g", P.dma("g", dbg_xnq, xnqT[:]), deps=[xnq_last], kind="dma")
            if "weff" in taps:
                dbg_weff = dt("dbg_weff", [128, 8, 1536], BF16, kind="ExternalOutput").ap()
                P.emit("g", P.dma("g", dbg_weff, weff[:]), deps=[weff_done], kind="dma")
            P.flush(blk)
            c1p.close()
        if upto < 4:
            raise _StopBuild()

        with nc.Block("p1b") as blk:
            e1b = c1.enter_context
            vstg = [e1b(nc.sbuf_tensor(f"vstg{i}", [128, 16, 260], BF16)) for i in range(2)]
            kstg = [e1b(nc.sbuf_tensor(f"kstg{i}", [64, T_], BF16)) for i in range(2)]
            pv = [e1b(nc.psum_tensor(f"pv{i}", [128, 1024], F32)) for i in range(2)]
            pk = [e1b(nc.psum_tensor(f"pk{i}", [64, 512], F32)) for i in range(2)]
            vst_ones = []
            for i in range(2):
                vst_ones.append(P.emit("g", lambda g, i=i: g.memset(
                    vstg[i][:].rearrange("p h (c o) -> p h c o", o=65)[:, :, :, 64:65],
                    1.0)))

            prev_pv = [None, None]
            vb_store = [None, None]

            def vbuild(ckv_src, nchunks, dall, dep_ck):
                nonlocal prev_pv
                last_store = vb_store
                final = None
                for grp in range(nchunks // 4):
                    gb = grp % 2
                    ev_last = None
                    for tc4 in range(4):
                        tc = grp * 4 + tc4
                        pb = tc % 2
                        mms = []
                        for k2 in range(2):
                            for n2 in range(2):
                                mms.append(P.emit("t",
                                    lambda t, k2=k2, n2=n2, tc=tc, pb=pb, ckv_src=ckv_src: t.matmul(
                                        pv[pb][:, 512 * n2:512 * n2 + 512],
                                        ckv_src[:, k2, 128 * tc:128 * tc + 128],
                                        wuv2[:, k2, 512 * n2:512 * n2 + 512],
                                        start=(k2 == 0), stop=(k2 == 1)),
                                    deps=[dep_ck, dwk["uv"], prev_pv[pb]]))
                        ev_last = P.emit("v", lambda v, tc4=tc4, gb=gb, pb=pb: v.tensor_copy(
                            vstg[gb][:].rearrange("p h (c o) -> p h c o", o=65)[:, :, tc4, 0:64],
                            pv[pb][:].rearrange("p (h o) -> p h o", o=64)),
                            deps=[mms[-1], vst_ones[gb], last_store[gb]])
                        prev_pv[pb] = ev_last
                    for h in range(H_):
                        final = P.emit("s", P.dma("s",
                            dall[h, :, grp, :], vstg[gb][:, h, :]),
                            deps=[ev_last], kind="dma")
                    last_store[gb] = final
                return final

            v_done = vbuild(ckv, 32, vall, ln_ck_done)
            vd_done = vbuild(ckvd, 16, vdall, ln_ck_done)

            prev_pk = [None, None]
            kb_store = [None, None]

            def kbuild(ckv_src, ntok, dall, krsrc, dep_ck, dep_kr):
                nonlocal prev_pk
                prev_dma = kb_store
                final = None
                for h in range(H_):
                    hb = h % 2
                    ev_last = None
                    for ns in range(ntok // 512):
                        pb = ns % 2
                        mms = []
                        for k2 in range(2):
                            mms.append(P.emit("t",
                                lambda t, k2=k2, ns=ns, h=h, pb=pb, ckv_src=ckv_src: t.matmul(
                                    pk[pb][:],
                                    wuk2[:, k2, 64 * h:64 * h + 64],
                                    ckv_src[:, k2, 512 * ns:512 * ns + 512],
                                    start=(k2 == 0), stop=(k2 == 1)),
                                deps=[dep_ck, dwk["uk"], prev_pk[pb]]))
                        ev_last = P.emit("v", lambda v, ns=ns, hb=hb, pb=pb: v.tensor_copy(
                            kstg[hb][:, 512 * ns:512 * ns + 512], pk[pb][:]),
                            deps=[mms[-1], prev_dma[hb]])
                        prev_pk[pb] = ev_last
                    d1 = P.emit("s", P.dma("s", dall[h, 0:64, 0:ntok], kstg[hb][:, 0:ntok]),
                                deps=[ev_last], kind="dma")
                    d2 = P.emit("s", P.dma("s", dall[h, 64:97, 0:ntok], krsrc[:, 0:ntok]),
                                deps=list(dep_kr), kind="dma")
                    prev_dma[hb] = d1
                    final = d2
                return final

            k_done = kbuild(ckv, T_, kall, kRf, ln_ck_done, [ln_kr_done, mone])
            kd_done = kbuild(ckvd, NQ, kdall, kRq, ln_ck_done, [ln_kr_done, mone2])
            P.flush(blk)
        c1.close()
        kv_scratch_done = [v_done, vd_done, k_done, kd_done]
        if upto < 5:
            raise _StopBuild()

        qall_stores_all = []
        ffn_stores = []
        with ExitStack() as c2, nc.Block("p2") as blk:
            e2 = c2.enter_context
            qstg = e2(nc.sbuf_tensor("qstg", [97, 16, 512], BF16))
            actblk = e2(nc.sbuf_tensor("actblk", [128, 32, 512], BF16))
            w1b = [e2(nc.sbuf_tensor(f"w1b{i}", [128, 8, 256], BF16)) for i in range(2)]
            w2b = [e2(nc.sbuf_tensor(f"w2b{i}", [128, 2, 1024], BF16)) for i in range(2)]
            scrA_ = [e2(nc.sbuf_tensor(f"scrA{i}", [128, 512], F32)) for i in range(2)]
            scrB_ = [e2(nc.sbuf_tensor(f"scrB{i}", [128, 512], F32)) for i in range(2)]
            scrI = e2(nc.sbuf_tensor("scrI", [128, 512], I32))
            scrC = e2(nc.sbuf_tensor("scrC", [128, 512], F32))
            otile = [e2(nc.sbuf_tensor(f"otile{i}", [128, 1024], F32)) for i in range(2)]
            pq = [e2(nc.psum_tensor(f"pq{i}", [128, 512], F32)) for i in range(2)]
            pw2 = [e2(nc.psum_tensor(f"pw2{i}", [128, 1024], F32)) for i in range(2)]

            if "qchain" in taps:
                dbgq = [dt(f"dbgq{i}", [128, 512], F32, kind="ExternalOutput").ap() for i in range(5)]
                dbgq8 = dt("dbgq8", [128, 512], BF16, kind="ExternalOutput").ap()
            last_pq = [weff_done, weff_done]
            last_w1l = [None, None]
            last_w2l = [None, None]
            prev_u = [None, None]
            prev_u2 = [None, None]
            qall_prev = []
            ffn_st = [None, None]
            ffn_ev = [None, None]
            prev_d7 = [None, None]
            last_w2ev = [None, None]
            mq_prev = None
            for bl in range(4):
                cols = slice(512 * bl, 512 * bl + 512)
                mq96 = P.emit("g", lambda g: g.memset(qstg[96:97, :, :], -10.0),
                              deps=qall_prev + [mq_prev])
                mq_prev = mq96
                st16 = []
                for qc in range(12):
                    qb = qc % 2
                    mms = []
                    for f in range(8):
                        mms.append(P.emit("t", lambda t, f=f, qc=qc, qb=qb, cols=cols: t.matmul(
                            pq[qb][:], weff[:, f, 128 * qc:128 * qc + 128],
                            xnqT[:, f, cols], start=(f == 0), stop=(f == 7)),
                            deps=[weff_done, xnq_last, last_pq[qb]]))
                    if qc < 8:
                        P.emit("v", lambda v, qc=qc, qb=qb: v.tensor_scalar(
                            qstg[0:64, 2 * qc, :], pq[qb][0:64, :], INV_SQ96, None, ALU.mult),
                            deps=[mms[-1]] + qall_prev)
                        last_pq[qb] = P.emit("v", lambda v, qc=qc, qb=qb: v.tensor_scalar(
                            qstg[0:64, 2 * qc + 1, :], pq[qb][64:128, :], INV_SQ96, None,
                            ALU.mult), deps=[mms[-1]])
                    else:
                        evl = None
                        for m in range(4):
                            h = 4 * (qc - 8) + m
                            evl = P.emit("v", lambda v, m=m, h=h, qb=qb, cols=cols: v.tensor_tensor(
                                qstg[64:96, h, :], pq[qb][32 * m:32 * m + 32, :],
                                scqrep[32 * m:32 * m + 32, cols], ALU.mult),
                                deps=[mms[-1], dreps[3]] + qall_prev)
                        last_pq[qb] = evl
                for h in range(H_):
                    st16.append(P.emit("s", P.dma("s",
                        qall[h, :, cols], qstg[:, h, :]),
                        deps=[last_pq[0], last_pq[1], mq96], kind="dma"))
                qall_prev = st16
                qall_stores_all += st16

                if p2sub < 2:
                    continue
                for op in range(16):
                    wb = op % 2
                    dlw = P.emit("y", P.dma("y", w1b[wb][:],
                        w1_s[:, 256 * op:256 * op + 256].rearrange("(f p) k -> p f k", p=128)),
                        deps=[last_w1l[wb]] + (w_stage_done if bl == 0 and op < 2 else []),
                        kind="dma")
                    last_w1l[wb] = None
                    for o2 in range(2):
                        o = 2 * op + o2
                        hb = o % 2
                        mms = []
                        for f in range(8):
                            mms.append(P.emit("t", lambda t, f=f, o2=o2, wb=wb, hb=hb, cols=cols: t.matmul(
                                pq[hb][:], w1b[wb][:, f, 128 * o2:128 * o2 + 128],
                                xnqT[:, f, cols], start=(f == 0), stop=(f == 7)),
                                deps=[dlw, xnq_last, last_pq[hb]]))
                        if p2sub == 21:
                            d7 = P.emit("s", lambda s, o=o, hb=hb: s.activation(
                                actblk[:, o, :], pq[hb][:], AF.Relu, bias=b1c[:, o:o + 1]),
                                deps=[mms[-1]])
                            last_pq[hb] = d7
                            prev_d7 = d7
                            last_w1l[wb] = mms[-1]
                            continue
                        sA, sB = scrA_[hb], scrB_[hb]
                        dc_ = P.emit("s", lambda s, o=o, hb=hb, sA=sA: s.activation(
                            sA[:], pq[hb][:], AF.Sin, scale=0.1,
                            bias=cosb[:, o:o + 1]), deps=[mms[-1], s_cosb, prev_d7[hb]])
                        ds_ = P.emit("s", lambda s, o=o, hb=hb, sB=sB: s.activation(
                            sB[:], pq[hb][:], AF.Sin, scale=0.1,
                            bias=sinb[:, o:o + 1]), deps=[mms[-1], prev_u2[hb]])
                        du = P.emit("v", lambda v, o=o, sA=sA: v.tensor_scalar(
                            sA[:], sA[:], ksin[:, o:o + 1], None, ALU.mult),
                            deps=[dc_])
                        du2 = P.emit("v", lambda v, o=o, sA=sA, sB=sB: v.scalar_tensor_tensor(
                            sB[:], sB[:], kcos[:, o:o + 1], sA[:],
                            ALU.mult, ALU.add), deps=[ds_, du])
                        prev_u[hb] = du2
                        d6 = P.emit("v", lambda v, hb=hb, sA=sA, sB=sB: v.tensor_tensor(
                            sA[:], sB[:], pq[hb][:], ALU.add), deps=[du2])
                        prev_u2[hb] = d6
                        last_pq[hb] = d6
                        d7 = P.emit("s", lambda s, o=o, sA=sA: s.activation(
                            actblk[:, o, :], sA[:], AF.Relu, bias=b1c[:, o:o + 1]),
                            deps=[d6] + ([last_w2ev[0], last_w2ev[1]] if op == 0 and o2 == 0 else []))
                        prev_d7[hb] = d7
                        last_w1l[wb] = mms[-1]
                w1_done = P.emit("v", lambda v: v.nop(), deps=[prev_d7[0], prev_d7[1]])
                if bl == 0 and "act0" in taps:
                    dbg_act0 = dt("dbg_act0", [128, 32, 512], BF16, kind="ExternalOutput").ap()
                    P.emit("g", P.dma("g", dbg_act0, actblk[:]), deps=[w1_done], kind="dma")

                if p2sub < 3:
                    continue
                for tp in range(2):
                    mmw = None
                    for op in range(16):
                        wb = op % 2
                        dlw = P.emit("y", P.dma("y", w2b[wb][:],
                            w2_s[256 * op:256 * op + 256, :].rearrange("(a p) n -> p a n", p=128)),
                            deps=[last_w2l[wb]] + (w_stage_done if bl == 0 and tp == 0 and op < 2 else []),
                            kind="dma")
                        for o2 in range(2):
                            o = 2 * op + o2
                            for ti in range(2):
                                ts4 = 2 * tp + ti
                                for n2 in range(2):
                                    mmw = P.emit("t",
                                        lambda t, o=o, o2=o2, ts4=ts4, ti=ti, n2=n2, wb=wb: t.matmul(
                                            pw2[ti][:, 512 * n2:512 * n2 + 512],
                                            actblk[:, o, 128 * ts4:128 * ts4 + 128],
                                            w2b[wb][:, o2, 512 * n2:512 * n2 + 512],
                                            start=(o == 0), stop=(o == 31)),
                                        deps=[dlw, w1_done]
                                        + ([last_w2ev[ti]] if o == 0 else []))
                        last_w2l[wb] = mmw
                    for ti in range(2):
                        ts4 = 2 * tp + ti
                        ob = ts4 % 2
                        dev = P.emit("v", lambda v, ti=ti, ob=ob: v.tensor_tensor(
                            otile[ob][:], pw2[ti][:], b2rep[:], ALU.add),
                            deps=[mmw, s_b2, ffn_st[ob]])
                        last_w2ev[ti] = dev
                        ffn_st[ob] = P.emit("y", P.dma("y",
                            ffn_s[512 * bl + 128 * ts4:512 * bl + 128 * ts4 + 128, :],
                            otile[ob][:]), deps=[dev], kind="dma")
                        ffn_stores.append(ffn_st[ob])
            if "act" in taps:
                dbg_act = dt("dbg_act", [128, 32, 512], BF16, kind="ExternalOutput").ap()
                P.emit("g", P.dma("g", dbg_act, actblk[:]), deps=[prev_d7], kind="dma")
            P.flush(blk)
        ctx12.close()
        if upto < 6:
            raise _StopBuild()

        ctx34 = ExitStack()
        e34 = ctx34.enter_context
        ycache = [e34(nc.sbuf_tensor(f"yc{j}", [128, NQ], F32)) for j in range(8)]

        with ExitStack() as c3, nc.Block("p3") as blk:
            e3 = c3.enter_context
            kh = [e3(nc.sbuf_tensor(f"kh{i}", [97, T_], BF16)) for i in range(2)]
            kdh = [e3(nc.sbuf_tensor(f"kdh{i}", [97, NQ], BF16)) for i in range(2)]
            vh = [e3(nc.sbuf_tensor(f"vh{i}", [128, 2144], BF16)) for i in range(2)]
            vdh = [e3(nc.sbuf_tensor(f"vdh{i}", [128, 1104], BF16)) for i in range(2)]
            qph = [e3(nc.sbuf_tensor(f"qph{i}", [97, NQ], BF16)) for i in range(2)]
            ptb = [e3(nc.sbuf_tensor(f"ptb{i}", [128, 1024], BF16)) for i in range(2)]
            dtmp = [e3(nc.sbuf_tensor(f"dtmp{i}", [1, NQ], F32)) for i in range(2)]
            ztile = e3(nc.sbuf_tensor("ztile", [128, 128], BF16))
            psc = [e3(nc.psum_tensor(f"psc{i}", [128, 1024], F32)) for i in range(2)]
            psy = e3(nc.psum_tensor("psy", [128, NQ], F32))

            zm = P.emit("g", lambda g: g.memset(ztile[:], 0.0))
            zc = P.gchk([zm])
            warm = None
            for p_ in range(NPOS):
                warm = P.emit("t", lambda t, p_=p_: t.matmul(
                    psy[:, 128 * p_:128 * p_ + 128], ztile[:], ztile[:],
                    start=True, stop=True), deps=[zc])
            HEAD_ORDER = [0] + list(range(H_))
            NH = len(HEAD_ORDER)
            head_done = [None] * NH
            yevac = [None] * NH
            denev = [None] * NH
            denst = [None] * NH
            for hi, h in enumerate(HEAD_ORDER):
                hb = hi % 2
                depf = head_done[hi - 2] if hi >= 2 else None
                dep0 = (kv_scratch_done + qall_stores_all) if hi == 0 else []
                l1 = P.emit("y", P.dma("y", kh[hb][:], kall[h, :, :]),
                            deps=dep0 + [depf], kind="dma")
                l2 = P.emit("y", P.dma("y", kdh[hb][:], kdall[h, :, :]),
                            deps=[depf], kind="dma")
                l3 = P.emit("y", P.dma("y",
                                       vh[hb][:, 0:2080].rearrange("p (g c) -> p g c", c=260),
                                       vall[h]),
                            deps=[depf], kind="dma")
                l4 = P.emit("y", P.dma("y",
                                       vdh[hb][:, 0:1040].rearrange("p (g c) -> p g c", c=260),
                                       vdall[h]),
                            deps=[depf], kind="dma")
                l5 = P.emit("y", P.dma("y", qph[hb][:], qall[h, :, :]),
                            deps=[depf], kind="dma")

                if hi == 1 and "p3in" in taps:
                    dbg_kh = dt("dbg_kh", [97, T_], BF16, kind="ExternalOutput").ap()
                    dbg_qh = dt("dbg_qh", [97, NQ], BF16, kind="ExternalOutput").ap()
                    dbg_vh = dt("dbg_vh", [128, 8, 260], BF16, kind="ExternalOutput").ap()
                    t1_ = P.emit("s", P.dma("s", dbg_kh, kh[0][:]), deps=[l1], kind="dma")
                    t2_ = P.emit("s", P.dma("s", dbg_qh, qph[0][:]), deps=[l5], kind="dma")
                    t3_ = P.emit("s", P.dma("s", dbg_vh, vh[0][:]), deps=[l3], kind="dma")
                    l1 = P.emit("g", lambda g: g.nop(), deps=[t1_, t2_, t3_])
                    l2 = l3 = l4 = l5 = l1
                exp_s = [None] * NGRP
                av_s = [None] * NGRP

                def emit_scores(gi):
                    kind, c, p0, ln, pad_abs = GROUPS[gi]
                    gb = gi % 2
                    mms = []
                    dep_psc = exp_s[gi - 2] if gi >= 2 else None
                    if kind == 0:
                        lhs = kh[hb][:, 128 * c:128 * c + 128]
                        for q0 in range(0, ln, 4):
                            w = min(4, ln - q0)
                            mms.append(P.emit("t",
                                lambda t, lhs=lhs, q0=q0, w=w, p0=p0, gb=gb, hb=hb: t.matmul(
                                    psc[gb][:, 128 * q0:128 * q0 + 128 * w], lhs,
                                    qph[hb][:, 128 * (p0 + q0):128 * (p0 + q0) + 128 * w],
                                    start=True, stop=True),
                                deps=[l1, l5, dep_psc]))
                    else:
                        for sl in range(ln):
                            p = p0 + sl
                            lhs = kdh[hb][:, 128 * p:128 * p + 128]
                            mms.append(P.emit("t", lambda t, lhs=lhs, p=p, sl=sl, gb=gb, hb=hb: t.matmul(
                                psc[gb][:, 128 * sl:128 * sl + 128], lhs,
                                qph[hb][:, 128 * p:128 * p + 128],
                                start=(sl % 4 == 0), stop=True),
                                deps=[l2, l5, dep_psc]))
                    vdep = None
                    if kind == 1:
                        vdep = P.emit("v", lambda v, gb=gb: v.tensor_tensor(
                            psc[gb][:], psc[gb][:], tri8b[:], ALU.add),
                            deps=[mms[-1], s_tri])
                    elif pad_abs is not None:
                        sl_loc = pad_abs - p0
                        vdep = P.emit("v", lambda v, gb=gb, sl_loc=sl_loc, pad_abs=pad_abs: v.tensor_scalar(
                            psc[gb][:, 128 * sl_loc:128 * sl_loc + 128],
                            psc[gb][:, 128 * sl_loc:128 * sl_loc + 128],
                            padb[:, pad_abs:pad_abs + 1], None, ALU.add),
                            deps=[mms[-1], s_pad])
                    dep_pt = av_s[gi - 2] if gi >= 2 else None
                    exp_s[gi] = P.emit("s", lambda s, gb=gb, ln=ln: s.activation(
                        ptb[gb][:, 0:128 * ln], psc[gb][:, 0:128 * ln], AF.Exp),
                        deps=[mms[-1], vdep, dep_pt])

                def emit_attv(gi):
                    kind, c, p0, ln, pad_abs = GROUPS[gi]
                    gb = gi % 2
                    avs = []
                    if kind == 0:
                        off = 260 * (c // 4) + 65 * (c % 4)
                        vap = vh[hb][:, off:off + 128][:, 0:65]
                        for q0 in range(0, ln, 4):
                            w = min(4, ln - q0)
                            first = (c == 0)
                            avs.append(P.emit("t",
                                lambda t, vap=vap, q0=q0, w=w, p0=p0, gb=gb, first=first: t.matmul(
                                    psy[0:65, 128 * (p0 + q0):128 * (p0 + q0) + 128 * w], vap,
                                    ptb[gb][:, 128 * q0:128 * q0 + 128 * w],
                                    start=first, stop=False),
                                deps=[exp_s[gi], l3, warm]
                                + ([yevac[hi - 1], denev[hi - 1]] if (gi == 0 and hi >= 1) else [])))
                    else:
                        for sl in range(ln):
                            p = p0 + sl
                            off = 260 * (p // 4) + 65 * (p % 4)
                            vap = vdh[hb][:, off:off + 128][:, 0:65]
                            avs.append(P.emit("t",
                                lambda t, vap=vap, p=p, sl=sl, gb=gb: t.matmul(
                                    psy[0:65, 128 * p:128 * p + 128], vap,
                                    ptb[gb][:, 128 * sl:128 * sl + 128],
                                    start=False, stop=True),
                                deps=[exp_s[gi], l4, warm]))
                    av_s[gi] = avs[-1]

                for gi in range(NGRP):
                    emit_scores(gi)
                    if gi >= 1:
                        emit_attv(gi - 1)
                emit_attv(NGRP - 1)
                head_done[hi] = av_s[-1]
                yevac[hi] = P.emit("v", lambda v, h=h: v.tensor_copy(
                    ycache[h // 2][64 * (h % 2):64 * (h % 2) + 64, :], psy[0:64, :]),
                    deps=[av_s[-1]])
                denev[hi] = P.emit("v", lambda v, h=h, hb=hb: v.tensor_copy(
                    dtmp[hb][:], psy[64:65, :]),
                    deps=[av_s[-1], denst[hi - 2] if hi >= 2 else None])
                denst[hi] = P.emit("s", P.dma("s", denh_s[h:h + 1, :], dtmp[hb][:]),
                                  deps=[denev[hi]], kind="dma")
            if "yc" in taps:
                dbg_yc = dt("dbg_yc", [128, 8, NQ], F32, kind="ExternalOutput").ap()
                for j in range(8):
                    P.emit("s", P.dma("s", dbg_yc[:, j, :], ycache[j][:]),
                           deps=list(yevac), kind="dma")
            P.flush(blk)

        if upto < 7:
            raise _StopBuild()
        with ExitStack() as c4, nc.Block("p4") as blk:
            e4 = c4.enter_context
            ybf = e4(nc.sbuf_tensor("ybf", [128, 8, NQ], BF16))
            wob = e4(nc.sbuf_tensor("wob", [128, 8, C_], BF16))
            dn128 = e4(nc.sbuf_tensor("dn128", [128, 256], F32))
            rrep = [e4(nc.sbuf_tensor("rrep0", [128, NQ], F32))] * 2
            xrt = [e4(nc.sbuf_tensor(f"xrt{i}", [128, C_], F32)) for i in range(2)]
            frt = [e4(nc.sbuf_tensor(f"frt{i}", [128, C_], F32)) for i in range(2)]
            ott = [e4(nc.sbuf_tensor(f"ott{i}", [128, C_], F32)) for i in range(2)]
            pwo = [e4(nc.psum_tensor(f"pwo{i}", [128, 1024], F32)) for i in range(2)]

            dwo = P.emit("g", P.dma("g", wob[:],
                                    W_O.rearrange("(j p) n -> p j n", p=128)), kind="dma")
            dwo = P.gchk([dwo])
            dr1 = P.emit("y", P.dma("y", dn128[:],
                                    denh_s.rearrange("h (a n) -> h a n", n=256)),
                         deps=list(denst), kind="dma")
            dr2 = P.emit("v", lambda v: v.reciprocal(dn128[:], dn128[:]), deps=[dr1])
            dr3 = P.emit("y", P.dma("y", denr_s.rearrange("h (a n) -> h a n", n=256),
                                    dn128[:]), deps=[dr2], kind="dma")
            norm = [None] * H_
            repl = [None, None]
            for h in range(H_):
                rb = h % 2
                repl[rb] = P.emit("y", P.dma("y", rrep[rb][:],
                                             denr_s[h:h + 1, :].partition_broadcast(128)),
                                  deps=[dr3, norm[h - 1] if h >= 1 else None], kind="dma")
                norm[h] = P.emit("v", lambda v, h=h, rb=rb: v.tensor_tensor(
                    ybf[64 * (h % 2):64 * (h % 2) + 64, h // 2, :],
                    ycache[h // 2][64 * (h % 2):64 * (h % 2) + 64, :],
                    rrep[rb][64 * (h % 2):64 * (h % 2) + 64, :], ALU.mult),
                    deps=[repl[rb], yevac[h + 1]])

            ost = [None, None]
            oev = [None, None]
            for pos in range(NPOS):
                pb = pos % 2
                dxr = P.emit("s", P.dma("s", xrt[pb][:], xq[128 * pos:128 * pos + 128, :]),
                             deps=[oev[pb]], kind="dma")
                dfr = P.emit("s", P.dma("s", frt[pb][:], ffn_s[128 * pos:128 * pos + 128, :]),
                             deps=[oev[pb]] + (ffn_stores if pos == 0 else []), kind="dma")
                mms = []
                for j in range(8):
                    for n2 in range(2):
                        mms.append(P.emit("t", lambda t, j=j, n2=n2, pos=pos, pb=pb: t.matmul(
                            pwo[pb][:, 512 * n2:512 * n2 + 512],
                            ybf[:, j, 128 * pos:128 * pos + 128],
                            wob[:, j, 512 * n2:512 * n2 + 512],
                            start=(j == 0), stop=(j == 7)),
                            deps=[dwo, norm[2 * j], norm[2 * j + 1]]
                            + ([oev[pb]] if j == 0 else [])))
                d1 = P.emit("v", lambda v, pb=pb: v.tensor_tensor(
                    ott[pb][:], pwo[pb][:], xrt[pb][:], ALU.add),
                    deps=[mms[-1], dxr, ost[pb]])
                d2 = P.emit("v", lambda v, pb=pb: v.tensor_tensor(
                    ott[pb][:], ott[pb][:], frt[pb][:], ALU.add), deps=[d1, dfr])
                oev[pb] = d2
                ost[pb] = P.emit("y", P.dma("y", out[128 * pos:128 * pos + 128, :],
                                            ott[pb][:]), deps=[d2], kind="dma")
            P.flush(blk)
        ctx34.close()

      _ = None
    except _StopBuild:
        pass
    return nc


_NC_CACHE = None


def _get_nc():
    global _NC_CACHE
    if _NC_CACHE is None:
        _NC_CACHE = build_nc()
    return _NC_CACHE


def _host_consts():
    c = {}
    c["identf"] = np.eye(128, dtype=np.float32)
    r = np.arange(128)
    tri = np.where(r[None, :] >= r[:, None], 0.0, NEG_BIG).astype(np.float32)
    c["tri8"] = np.ascontiguousarray(np.tile(tri, (1, 8)).astype(np.float32))
    c["trowf"] = np.ascontiguousarray(np.tile(np.arange(T_, dtype=np.float32), (32, 1)))
    c["jcol"] = ((np.arange(32) % 16) / 16.0).astype(np.float32).reshape(32, 1)
    return c


def kernel(L0_attn_x, L0_attn_W_DQ, L0_attn_W_UQ, L0_attn_W_DKV, L0_attn_W_UK,
           L0_attn_W_UV, L0_attn_W_QR, L0_attn_W_KR, L0_attn_W_O,
           L0_ffn_W1, L0_ffn_b1, L0_ffn_quantum_phases,
           L0_ffn_quantum_weights_real, L0_ffn_quantum_weights_imag,
           L0_ffn_W2, L0_ffn_b2):
    x = np.ascontiguousarray(np.asarray(L0_attn_x, dtype=np.float32))
    consts = _host_consts()
    shared = dict(
        W_DQ=np.ascontiguousarray(L0_attn_W_DQ, dtype=np.float32),
        W_UQ=np.ascontiguousarray(L0_attn_W_UQ, dtype=np.float32),
        W_QR=np.ascontiguousarray(L0_attn_W_QR, dtype=np.float32),
        W_DKV=np.ascontiguousarray(L0_attn_W_DKV, dtype=np.float32),
        W_UK=np.ascontiguousarray(L0_attn_W_UK, dtype=np.float32),
        W_UV=np.ascontiguousarray(L0_attn_W_UV, dtype=np.float32),
        WKRpt=np.ascontiguousarray(
            np.asarray(L0_attn_W_KR, dtype=np.float32).reshape(8, 128, 32).transpose(1, 0, 2)),
        W_O=np.ascontiguousarray(L0_attn_W_O, dtype=np.float32),
        W1=np.ascontiguousarray(L0_ffn_W1, dtype=np.float32),
        W2=np.ascontiguousarray(L0_ffn_W2, dtype=np.float32),
        b1pt=np.ascontiguousarray(np.asarray(L0_ffn_b1, dtype=np.float32).reshape(32, 128).T),
        thpt=np.ascontiguousarray(
            np.asarray(L0_ffn_quantum_phases, dtype=np.float32).reshape(32, 128).T),
        wrpt=np.ascontiguousarray(
            np.asarray(L0_ffn_quantum_weights_real, dtype=np.float32).reshape(32, 128).T),
        wipt=np.ascontiguousarray(
            np.asarray(L0_ffn_quantum_weights_imag, dtype=np.float32).reshape(32, 128).T),
        b2row=np.ascontiguousarray(np.asarray(L0_ffn_b2, dtype=np.float32).reshape(1, C_)),
        **consts,
    )
    in_maps = []
    qrows_all = []
    for core in range(8):
        b = core // 2
        tiles = TILES_A if core % 2 == 0 else TILES_B
        qrows = np.concatenate([np.arange(128 * j, 128 * j + 128) for j in tiles])
        qrows_all.append(qrows)
        padv = np.array([0.0 if tiles[p] == C_P[p] - 1 else NEG_BIG
                         for p in range(NPOS)], dtype=np.float32)
        im = dict(shared)
        im["xfull"] = np.ascontiguousarray(x[b])
        im["xq"] = np.ascontiguousarray(x[b][qrows])
        im["padbin"] = np.ascontiguousarray(np.tile(padv[None, :], (128, 1)).astype(np.float32))
        im["trowq"] = np.ascontiguousarray(np.tile(qrows.astype(np.float32)[None, :], (32, 1)))
        in_maps.append(im)
    nc = _get_nc()
    res = run_bass_kernel_spmd(nc, in_maps, core_ids=list(range(8)))
    full = np.empty((B_, T_, C_), dtype=np.float32)
    for core in range(8):
        full[core // 2][qrows_all[core]] = np.asarray(res.results[core]["out"])
    return full


# version 08
# speedup vs baseline: 1.1754x; 1.0178x over previous
import numpy as np
import concourse.bass as bass
import concourse.mybir as mybir
from concourse.bass_utils import run_bass_kernel_spmd
from contextlib import ExitStack

F32 = mybir.dt.float32
BF16 = mybir.dt.bfloat16
I32 = mybir.dt.int32
AF = mybir.ActivationFunctionType
ALU = mybir.AluOpType
AX = mybir.AxisListType

B_, T_, C_ = 4, 4096, 1024
H_ = 16
DFF_ = 4096
NQ = 2048
NPOS = 16
NCH = 32
C_P = [32 - 2 * p for p in range(NPOS)]
N_SLOTS = [sum(1 for p in range(NPOS) if C_P[p] - 2 >= c) for c in range(31)]
TWO_PI = float(2.0 * np.pi)
INV_SQ96 = float(1.0 / np.sqrt(96.0))
SQ2 = float(np.sqrt(2.0))
NEG_BIG = -30000.0

TILES_A = sorted([j for j in range(32) if j % 4 in (0, 3)], reverse=True)
TILES_B = sorted([j for j in range(32) if j % 4 in (1, 2)], reverse=True)
for _tiles in (TILES_A, TILES_B):
    for _p, _j in enumerate(_tiles):
        assert _j in (C_P[_p] - 1, C_P[_p] - 2) and C_P[_p] - 2 >= _j - 1
assert sorted(TILES_A + TILES_B) == list(range(32))

GROUPS = []
for _c in range(31):
    _n = N_SLOTS[_c]
    _p0 = 0
    while _p0 < _n:
        _ln = min(8, _n - _p0)
        _pad = (_n - 1) if (_c % 2 == 0 and _p0 <= _n - 1 < _p0 + _ln) else None
        GROUPS.append((0, _c, _p0, _ln, _pad))
        _p0 += _ln
GROUPS.append((1, -1, 0, 8, None))
GROUPS.append((1, -1, 8, 8, None))
NGRP = len(GROUPS)


class Prog:

    SECT = {"t": "tensor", "v": "vector", "s": "scalar", "y": "sync", "g": "gpsimd"}

    NDSEM = 48

    def __init__(self, nc, ctx):
        self.nc = nc
        self.sems = {e: ctx.enter_context(nc.semaphore("sem_" + e)) for e in self.SECT}
        self.cnt = dict.fromkeys(self.SECT, 0)
        self.pend = {e: [] for e in self.SECT}
        self.seen = {}
        self.steps = []
        self.dsems = [ctx.enter_context(nc.semaphore(f"dq{i}")) for i in range(self.NDSEM)]
        self.duse = [0] * self.NDSEM
        self.dnext = 0

    def emit(self, e, fn, deps=(), kind="op", incs=1):
        waits = []
        nd = False
        for d in deps:
            if d is None:
                continue
            pe, pv = self.steps[d]
            if pe == e:
                if kind == "op" and self.cnt[e] + 1 - pv <= 1:
                    nd = True
                continue
            key = (e, pe)
            if self.seen.get(key, 0) >= pv:
                continue
            self.seen[key] = pv
            waits.append(((self.sems[pe] if not isinstance(pe, tuple)
                           else self.dsems[pe[1]]), pv))
        sid = len(self.steps)
        if kind == "dma":
            i = self.dnext
            self.dnext = (self.dnext + 1) % self.NDSEM
            self.duse[i] += 1
            self.steps.append((("d", i), 16 * self.duse[i]))
            self.pend[e].append((waits, fn, kind, self.dsems[i], False))
        else:
            self.cnt[e] += 1
            self.steps.append((e, self.cnt[e]))
            self.pend[e].append((waits, fn, kind, None, nd))
        return sid

    def dma(self, e, out, in_):
        def fn(eng, sem, out=out, in_=in_):
            eng.dma_start(out, in_).then_inc(sem, 16)

        return fn

    def gchk(self, deps=()):
        def fn(g):
            g.drain(fusable=False)
            return g.nop()
        return self.emit("g", fn, deps=deps, kind="op")

    def drain(self, e):
        pairs = [(self.dsems[i], 16 * self.duse[i])
                 for i in range(self.NDSEM) if self.duse[i] > 0]

        def fn(eng, pairs=pairs):
            for sem, val in pairs:
                eng.wait_ge(sem, val)
            return eng.nop()

        return self.emit(e, fn, kind="op")

    def flush(self, block, drain=("y", "g", "s")):
        for e in drain:
            if self.pend[e]:
                if e == "g":
                    self.gchk()
                self.drain(e)
        for e in self.SECT:
            lst = self.pend[e]
            if not lst:
                continue
            sem = self.sems[e]

            def body(eng, lst=list(lst), sem=sem, e=e):
                for waits, fn, kind, dsem, nd in lst:
                    for (sh, v) in waits:
                        eng.wait_ge(sh, v)
                    if nd and e in ("v", "s"):
                        eng.drain(fusable=False)
                    if kind == "dma":
                        fn(eng, dsem)
                    else:
                        fn(eng).then_inc(sem, 1)

            getattr(block, self.SECT[e])(body)
            self.pend[e] = []


class _StopBuild(Exception):
    pass


def build_nc(upto=99, taps=(), p2sub=3):
    nc = bass.Bass("TRN2", target_bir_lowering=False)
    _dt0 = nc.dram_tensor

    def dt(name, shape, dtype, kind=None):
        if kind is None:
            kind = "ExternalOutput" if name in taps else "Internal"
        return _dt0(name, shape, dtype, kind=kind)

    xfull = dt("xfull", [T_, C_], F32, kind="ExternalInput").ap()
    xq = dt("xq", [NQ, C_], F32, kind="ExternalInput").ap()
    W_DQ = dt("W_DQ", [C_, DFF_], F32, kind="ExternalInput").ap()
    W_UQ = dt("W_UQ", [DFF_, C_], F32, kind="ExternalInput").ap()
    W_QR = dt("W_QR", [DFF_, 512], F32, kind="ExternalInput").ap()
    W_DKV = dt("W_DKV", [C_, 256], F32, kind="ExternalInput").ap()
    W_UK = dt("W_UK", [256, C_], F32, kind="ExternalInput").ap()
    W_UV = dt("W_UV", [256, C_], F32, kind="ExternalInput").ap()
    WKRpt = dt("WKRpt", [128, 8, 32], F32, kind="ExternalInput").ap()
    W_O = dt("W_O", [C_, C_], F32, kind="ExternalInput").ap()
    W1 = dt("W1", [C_, DFF_], F32, kind="ExternalInput").ap()
    W2 = dt("W2", [DFF_, C_], F32, kind="ExternalInput").ap()
    b1pt = dt("b1pt", [128, 32], F32, kind="ExternalInput").ap()
    thpt = dt("thpt", [128, 32], F32, kind="ExternalInput").ap()
    wrpt = dt("wrpt", [128, 32], F32, kind="ExternalInput").ap()
    wipt = dt("wipt", [128, 32], F32, kind="ExternalInput").ap()
    b2row = dt("b2row", [1, C_], F32, kind="ExternalInput").ap()
    identf = dt("identf", [128, 128], F32, kind="ExternalInput").ap()
    tri8 = dt("tri8", [128, 1024], F32, kind="ExternalInput").ap()
    padbin = dt("padbin", [128, 16], F32, kind="ExternalInput").ap()
    trowq = dt("trowq", [32, NQ], F32, kind="ExternalInput").ap()
    trowf = dt("trowf", [32, T_], F32, kind="ExternalInput").ap()
    jcol = dt("jcol", [32, 1], F32, kind="ExternalInput").ap()

    out = dt("out", [NQ, C_], F32, kind="ExternalOutput").ap()

    wdqT_s = dt("wdqT_s", [DFF_, C_], BF16).ap()
    wu_s = dt("wu_s", [DFF_, 1536], BF16).ap()
    w1_s = dt("w1_s", [C_, DFF_], BF16).ap()
    w2_s = dt("w2_s", [DFF_, C_], BF16).ap()
    kall = dt("kall", [H_, 97, T_], BF16).ap()
    kdall = dt("kdall", [H_, 97, NQ], BF16).ap()
    vall = dt("vall", [H_, 128, 8, 260], BF16).ap()
    vdall = dt("vdall", [H_, 128, 4, 260], BF16).ap()
    qall = dt("qall", [H_, 97, NQ], BF16).ap()
    ffn_s = dt("ffn_s", [NQ, C_], F32).ap()
    denr_s = dt("denr_s", [16, NQ], F32).ap()
    denh_s = dt("denh_s", [16, NQ], F32).ap()

    ctx = ExitStack()
    try:
      with ctx:
        e = ctx.enter_context
        P = Prog(nc, ctx)

        identb = e(nc.sbuf_tensor("identb", [128, 128], BF16))
        tri8b = e(nc.sbuf_tensor("tri8b", [128, 1024], F32))
        padb = e(nc.sbuf_tensor("padb", [128, 16], F32))
        b2rep = e(nc.sbuf_tensor("b2rep", [128, C_], F32))
        b1c = e(nc.sbuf_tensor("b1c", [128, 32], F32))
        th2pi = e(nc.sbuf_tensor("th2pi", [128, 32], F32))
        thb1 = e(nc.sbuf_tensor("thb1", [128, 32], F32))
        kq = e(nc.sbuf_tensor("kq", [128, 32], F32))
        ksin = e(nc.sbuf_tensor("ksin", [128, 32], F32))
        kcos = e(nc.sbuf_tensor("kcos", [128, 32], F32))
        sinb = e(nc.sbuf_tensor("sinb", [128, 32], F32))
        cosb = e(nc.sbuf_tensor("cosb", [128, 32], F32))
        scfull = e(nc.sbuf_tensor("scfull", [32, T_], F32))
        scqk = e(nc.sbuf_tensor("scqk", [32, NQ], F32))
        scqrep = e(nc.sbuf_tensor("scqrep", [128, NQ], F32))
        epsc = e(nc.sbuf_tensor("epsc", [128, 1], F32))

        ctx12 = ExitStack()
        e12 = ctx12.enter_context
        xnqT = e12(nc.sbuf_tensor("xnqT", [128, 8, NQ], BF16))
        weff = e12(nc.sbuf_tensor("weff", [128, 8, 1536], BF16))

        with ExitStack() as c0, nc.Block("p0") as blk:
            e0 = c0.enter_context
            thc = e0(nc.sbuf_tensor("thc", [128, 32], F32))
            wrc = e0(nc.sbuf_tensor("wrc", [128, 32], F32))
            wic = e0(nc.sbuf_tensor("wic", [128, 32], F32))
            jc = e0(nc.sbuf_tensor("jc", [32, 1], F32))
            invc = e0(nc.sbuf_tensor("invc", [32, 1], F32))
            trqT = e0(nc.sbuf_tensor("trqT", [32, NQ], F32))
            trfT = e0(nc.sbuf_tensor("trfT", [32, T_], F32))
            sq32 = e0(nc.sbuf_tensor("sq32", [32, T_], F32))
            sqi32 = e0(nc.sbuf_tensor("sqi32", [32, T_], I32))
            q32 = e0(nc.sbuf_tensor("q32", [128, 32], F32))
            qi32 = e0(nc.sbuf_tensor("qi32", [128, 32], I32))
            q32b = e0(nc.sbuf_tensor("q32b", [128, 32], F32))
            t1 = e0(nc.sbuf_tensor("t1c", [128, 32], F32))
            t2 = e0(nc.sbuf_tensor("t2c", [128, 32], F32))

            ld = {}
            for nm, dst, src in [
                ("ident", identb[:], identf), ("tri", tri8b[:], tri8),
                ("pad", padb[:], padbin), ("b1", b1c[:], b1pt),
                ("th", thc[:], thpt), ("wr", wrc[:], wrpt), ("wi", wic[:], wipt),
                ("jc", jc[:], jcol), ("trq", trqT[:], trowq), ("trf", trfT[:], trowf),
                ("b2", b2rep[:], b2row.partition_broadcast(128)),
            ]:
                ld[nm] = P.emit("g", P.dma("g", dst, src), kind="dma")
            s_eps = P.emit("g", lambda g: g.memset(epsc[:], 1e-5))
            chk0 = P.gchk(list(ld.values()) + [s_eps])
            ld = {k: chk0 for k in ld}
            s_eps = chk0

            s_inv = P.emit("s", lambda s: s.activation(
                invc[:], jc[:], AF.Exp, scale=float(-np.log(10000.0))),
                deps=[ld["jc"]])

            def rope_scale(tr, n, outt, mul, deps):
                dfr = P.emit("v", lambda v: v.tensor_scalar(
                    sq32[:, 0:n], tr, invc[:], None, ALU.mult),
                    deps=list(deps) + [s_inv])
                dz = P.emit("v", lambda v: v.tensor_scalar(
                    sq32[:, 0:n], sq32[:, 0:n], 1.0 / TWO_PI, 0.125, ALU.mult, ALU.add),
                    deps=[dfr])
                di = P.emit("v", lambda v: v.tensor_copy(sqi32[:, 0:n], sq32[:, 0:n]),
                            deps=[dz])
                dd = P.emit("v", lambda v: v.scalar_tensor_tensor(
                    sq32[:, 0:n], sqi32[:, 0:n], -1.0, sq32[:, 0:n], ALU.mult, ALU.add),
                    deps=[di])
                ds = P.emit("s", lambda s: s.activation(
                    sq32[:, 0:n], sq32[:, 0:n], AF.Sin, scale=TWO_PI), deps=[dd])
                dm = P.emit("v", lambda v: v.tensor_scalar(
                    outt, sq32[:, 0:n], float(SQ2 * mul), None, ALU.mult), deps=[ds])
                return dm, ds

            dscf, dsf_s = rope_scale(trfT[:], T_, scfull[:], 1.0, [ld["trf"]])
            dscq, dsq_s = rope_scale(trqT[:], NQ, scqk[:], 1.0, [ld["trq"], dsf_s])
            dreps = []
            for m in range(4):
                dreps.append(P.emit("v", lambda v, m=m: v.tensor_scalar(
                    scqrep[32 * m:32 * m + 32, :], scqk[:], INV_SQ96, None, ALU.mult),
                    deps=[dscq]))

            dth = P.emit("v", lambda v: v.tensor_scalar(
                th2pi[:], thc[:], 1.0 / TWO_PI, None, ALU.mult), deps=[ld["th"]])
            dthb = P.emit("v", lambda v: v.scalar_tensor_tensor(
                thb1[:], b1c[:], 0.1 / TWO_PI, th2pi[:], ALU.mult, ALU.add),
                deps=[dth, ld["b1"]])
            dsb = P.emit("v", lambda v: v.tensor_scalar(
                sinb[:], b1c[:], 0.1, None, ALU.mult), deps=[ld["b1"]])
            s_cosb = P.emit("v", lambda v: v.tensor_scalar(
                cosb[:], b1c[:], 0.1, float(np.pi / 2), ALU.mult, ALU.add),
                deps=[ld["b1"], dsb])

            def redsin(zadd, outt, deps):
                dz = P.emit("v", lambda v: v.tensor_scalar(
                    q32[:], th2pi[:], 1.0, zadd, ALU.mult, ALU.add), deps=deps)
                di = P.emit("v", lambda v: v.tensor_copy(qi32[:], q32[:]), deps=[dz])
                dd = P.emit("v", lambda v: v.scalar_tensor_tensor(
                    q32b[:], qi32[:], -1.0, q32[:], ALU.mult, ALU.add), deps=[di])
                ds = P.emit("s", lambda s: s.activation(
                    outt, q32b[:], AF.Sin, scale=TWO_PI), deps=[dd])
                return ds

            dsin = redsin(0.0, t1[:], [dth])
            dcos = redsin(0.25, t2[:], [dth, dsin])
            da = P.emit("v", lambda v: v.tensor_tensor(t2[:], t2[:], wrc[:], ALU.mult),
                        deps=[dcos, ld["wr"]])
            db = P.emit("v", lambda v: v.tensor_tensor(t1[:], t1[:], wic[:], ALU.mult),
                        deps=[dsin, ld["wi"]])
            dc2 = P.emit("v", lambda v: v.tensor_tensor(t1[:], t1[:], t2[:], ALU.add),
                         deps=[da, db])
            dc3 = P.emit("v", lambda v: v.tensor_tensor(t1[:], t1[:], wic[:], ALU.mult),
                         deps=[dc2])
            dkq = P.emit("v", lambda v: v.tensor_scalar(
                kq[:], t1[:], 0.1, None, ALU.mult), deps=[dc3])
            dsin2 = redsin(0.0, t1[:], [dth, dkq])
            dcos2 = redsin(0.25, t2[:], [dth, dsin2])
            dks = P.emit("v", lambda v: v.tensor_tensor(ksin[:], kq[:], t1[:], ALU.mult),
                         deps=[dsin2, dkq])
            dkc = P.emit("v", lambda v: v.tensor_tensor(kcos[:], kq[:], t2[:], ALU.mult),
                         deps=[dcos2, dks])
            dkq = dkc
            s_ident = ld["ident"]
            s_tri = ld["tri"]
            s_pad = ld["pad"]
            s_b2 = ld["b2"]
            P.flush(blk)
        s_consts = dkq
        if upto < 1:
            raise _StopBuild()

        with ExitStack() as cA, nc.Block("p0ba") as blk:
            eA = cA.enter_context
            wdqb = [eA(nc.sbuf_tensor(f"wdqb{i}", [128, 8, 256], BF16)) for i in range(2)]
            tstg = [eA(nc.sbuf_tensor(f"tstg{i}", [128, 2048], BF16)) for i in range(2)]
            ptrp = eA(nc.psum_tensor("ptrp", [128, 2048], BF16))
            last_store = [None, None]
            last_use = [None, None]
            prev_ev = None
            st = None
            for kp in range(16):
                bi = kp % 2
                dl = P.emit("g", P.dma("g", wdqb[bi][:],
                    W_DQ[:, 256 * kp:256 * kp + 256].rearrange("(f p) k -> p f k", p=128)),
                    deps=[last_use[bi]], kind="dma")
                dl = P.gchk([dl])
                trs = []
                for a in range(2):
                    for i in range(8):
                        trs.append(P.emit("t", lambda t, a=a, i=i, bi=bi: t.transpose(
                            ptrp[:, a * 1024 + i * 128: a * 1024 + i * 128 + 128],
                            wdqb[bi][:, i, a * 128:a * 128 + 128], identb[:]),
                            deps=[dl, s_ident] + ([prev_ev] if (a == 0 and i == 0) else [])))
                last_use[bi] = trs[-1]
                ev = P.emit("v", lambda v, bi=bi: v.tensor_copy(tstg[bi][:], ptrp[:]),
                            deps=[trs[-1], last_store[bi]])
                prev_ev = ev
                st = P.emit("s", P.dma("s",
                    wdqT_s[256 * kp:256 * kp + 256, :].rearrange("(a p) i -> p a i", p=128),
                    tstg[bi][:].rearrange("p (a i) -> p a i", a=2)),
                    deps=[ev], kind="dma")
                last_store[bi] = st
            wdqT_done = st
            P.flush(blk)
        if upto < 2:
            raise _StopBuild()

        with ExitStack() as cB, nc.Block("p0bb") as blk:
            eB = cB.enter_context
            stg = [eB(nc.sbuf_tensor(f"wustg{i}", [128, 8, 1024], BF16)) for i in range(2)]
            last_st = [None, None]

            pend_stage = []

            def stage(srcap, dstap, bi, cols=1024):
                dlw = P.emit("g", P.dma("g", stg[bi][:, :, 0:cols], srcap),
                             deps=[last_st[bi]], kind="dma")
                pend_stage.append((dlw, dstap, bi, cols))
                if len(pend_stage) == 2:
                    chk = P.gchk([pend_stage[0][0], pend_stage[1][0]])
                    for _, dap, bj, cj in pend_stage:
                        last_st[bj] = P.emit("s", P.dma("s", dap, stg[bj][:, :, 0:cj]),
                                             deps=[chk], kind="dma")
                    pend_stage.clear()
                return last_st[bi]

            for sl in range(4):
                stage(W_UQ[1024 * sl:1024 * sl + 1024, :].rearrange("(f p) n -> p f n", p=128),
                      wu_s[1024 * sl:1024 * sl + 1024, 0:1024].rearrange("(f p) n -> p f n", p=128),
                      sl % 2)
            for sl in range(4):
                stage(W_QR[1024 * sl:1024 * sl + 1024, :].rearrange("(f p) n -> p f n", p=128),
                      wu_s[1024 * sl:1024 * sl + 1024, 1024:1536].rearrange("(f p) n -> p f n", p=128),
                      sl % 2, cols=512)
            for sl in range(4):
                stage(W1[:, 1024 * sl:1024 * sl + 1024].rearrange("(f p) n -> p f n", p=128),
                      w1_s[:, 1024 * sl:1024 * sl + 1024].rearrange("(f p) n -> p f n", p=128),
                      sl % 2)
            for sl in range(4):
                stage(W2[1024 * sl:1024 * sl + 1024, :].rearrange("(f p) n -> p f n", p=128),
                      w2_s[1024 * sl:1024 * sl + 1024, :].rearrange("(f p) n -> p f n", p=128),
                      sl % 2)
            wu_done = last_st[0]
            w_stage_done = [last_st[0], last_st[1]]

            wdqTk = [eB(nc.sbuf_tensor(f"wdqTk{i}", [128, 1024], BF16)) for i in range(2)]
            wuk = [eB(nc.sbuf_tensor(f"wuk{i}", [128, 512], BF16)) for i in range(2)]
            pacc = [eB(nc.psum_tensor(f"pacc{i}", [128, 512], F32)) for i in range(8)]
            last_mm = [None, None]
            prev_evac = None
            for js in range(3):
                mm_last = None
                for k in range(NCH):
                    bi = k % 2
                    dA = P.emit("y", P.dma("y", wdqTk[bi][:], wdqT_s[128 * k:128 * k + 128, :]),
                                deps=[wdqT_done, last_mm[bi]], kind="dma")
                    dB = P.emit("y", P.dma("y", wuk[bi][:],
                        wu_s[128 * k:128 * k + 128, 512 * js:512 * js + 512]),
                        deps=w_stage_done + [last_mm[bi]], kind="dma")
                    mms = []
                    for i in range(8):
                        mms.append(P.emit("t", lambda t, i=i, k=k, bi=bi: t.matmul(
                            pacc[i][:], wdqTk[bi][:, 128 * i:128 * i + 128], wuk[bi][:],
                            start=(k == 0), stop=(k == NCH - 1)),
                            deps=[dA, dB] + ([prev_evac] if k == 0 else [])))
                    mm_last = mms[-1]
                    last_mm[bi] = mm_last
                for i in range(8):
                    prev_evac = P.emit("v", lambda v, i=i, js=js: v.tensor_copy(
                        weff[:, i, 512 * js:512 * js + 512], pacc[i][:]), deps=[mm_last])
            weff_done = prev_evac
            P.flush(blk)
        if upto < 3:
            raise _StopBuild()

        c1 = ExitStack()
        with nc.Block("p1") as blk:
            e1 = c1.enter_context
            wdkv = e1(nc.sbuf_tensor("wdkv", [128, 8, 256], BF16))
            wuk2 = e1(nc.sbuf_tensor("wuk2", [128, 2, 1024], BF16))
            wuv2 = e1(nc.sbuf_tensor("wuv2", [128, 2, 1024], BF16))
            wkr = e1(nc.sbuf_tensor("wkr", [128, 8, 32], BF16))
            ckv = e1(nc.sbuf_tensor("ckv", [128, 2, T_], BF16))
            ckvd = e1(nc.sbuf_tensor("ckvd", [128, 2, NQ], BF16))
            kRf = e1(nc.sbuf_tensor("kRf", [33, T_], BF16))
            kRq = e1(nc.sbuf_tensor("kRq", [33, NQ], BF16))
            xt = [e1(nc.sbuf_tensor(f"xt{i}", [128, C_], F32)) for i in range(2)]
            xn = [e1(nc.sbuf_tensor(f"xn{i}", [128, C_], BF16)) for i in range(2)]
            xnT = [e1(nc.sbuf_tensor(f"xnT{i}", [128, 8, 128], BF16)) for i in range(2)]
            scr = e1(nc.sbuf_tensor("scr1", [128, C_], F32))
            stat = e1(nc.sbuf_tensor("stat", [128, 6], F32))
            c1p = ExitStack()
            ptr1 = c1p.enter_context(nc.psum_tensor("ptr1", [128, 1024], BF16))
            pckv = c1p.enter_context(nc.psum_tensor("pckv", [128, 256], F32))
            pkr = c1p.enter_context(nc.psum_tensor("pkr", [32, 128], F32))

            dwk = {}
            dwk["dkv"] = P.emit("g", P.dma("g", wdkv[:],
                W_DKV.rearrange("(f p) k -> p f k", p=128)), kind="dma")
            dwk["uk"] = P.emit("g", P.dma("g", wuk2[:],
                W_UK.rearrange("(a p) n -> p a n", p=128)), kind="dma")
            dwk["uv"] = P.emit("g", P.dma("g", wuv2[:],
                W_UV.rearrange("(a p) n -> p a n", p=128)), kind="dma")
            dwk["kr"] = P.emit("g", P.dma("g", wkr[:], WKRpt), kind="dma")
            chk1 = P.gchk(list(dwk.values()))
            dwk = {k: chk1 for k in dwk}
            mone = P.emit("g", lambda g: g.memset(kRf[32:33, :], 1.0))
            mone2 = P.emit("g", lambda g: g.memset(kRq[32:33, :], 1.0))

            free_x = [None, None]
            free_xn = [None, None]
            free_xnT = [None, None]
            prev_ck = None
            prev_kr = None
            prev_tr = None
            prev_stat = None
            prev_var = None
            xnq_last = None
            for it in range(48):
                bi = it % 2
                is_q = it >= 32
                pos = it - 32
                src = (xq[128 * pos:128 * pos + 128, :] if is_q
                       else xfull[128 * it:128 * it + 128, :])
                dx = P.emit("y", P.dma("y", xt[bi][:], src),
                            deps=[free_x[bi]], kind="dma")
                P.emit("v", lambda v: v.memzero(stat[:, 0:2]), deps=[prev_stat])
                dsum = P.emit("v", lambda v, bi=bi: v.reduce_sum(
                    stat[:, 0:1], xt[bi][:], axis=AX.X), deps=[dx, prev_stat])
                dsq0 = P.emit("s", lambda s, bi=bi: s.activation(
                    scr[:], xt[bi][:], AF.Square), deps=[dx, prev_var])
                dsq = P.emit("v", lambda v: v.reduce_sum(
                    stat[:, 1:2], scr[:], axis=AX.X), deps=[dsq0])
                dmean = P.emit("v", lambda v: v.tensor_scalar(
                    stat[:, 2:3], stat[:, 0:1], 1.0 / 1024.0, None, ALU.mult),
                    deps=[dsum])
                dvar = P.emit("v", lambda v: v.scalar_tensor_tensor(
                    stat[:, 3:4], stat[:, 2:3], -1.0, stat[:, 0:1], ALU.mult, ALU.mult),
                    deps=[dmean, dsq])
                dvar2 = P.emit("v", lambda v: v.tensor_tensor(
                    stat[:, 3:4], stat[:, 1:2], stat[:, 3:4], ALU.add), deps=[dvar])
                prev_var = dvar2
                dsd = P.emit("s", lambda s: s.activation(
                    stat[:, 4:5], stat[:, 3:4], AF.Sqrt, scale=1.0 / 1024.0,
                    bias=epsc[:]), deps=[dvar2, s_eps])
                drs = P.emit("v", lambda v: v.reciprocal(stat[:, 5:6], stat[:, 4:5]),
                             deps=[dsd])
                dap = P.emit("v", lambda v, bi=bi: v.tensor_scalar(
                    xn[bi][:], xt[bi][:], stat[:, 2:3], stat[:, 5:6],
                    ALU.subtract, ALU.mult), deps=[drs, free_xn[bi]])
                prev_stat = dap
                free_x[bi] = dap
                trs = []
                for f in range(8):
                    trs.append(P.emit("t", lambda t, f=f, bi=bi: t.transpose(
                        ptr1[:, 128 * f:128 * f + 128], xn[bi][:, 128 * f:128 * f + 128],
                        identb[:]), deps=[dap, s_ident] + ([prev_tr] if f == 0 else [])))
                dev = P.emit("v", lambda v, bi=bi: v.tensor_copy(
                    xnT[bi][:].rearrange("p f k -> p (f k)"), ptr1[:]),
                    deps=[trs[-1], free_xnT[bi]])
                prev_tr = dev
                free_xn[bi] = dev
                if is_q:
                    xnq_last = P.emit("v", lambda v, bi=bi, pos=pos: v.tensor_copy(
                        xnqT[:, :, 128 * pos:128 * pos + 128], xnT[bi][:]), deps=[dev])
                mm1 = []
                for k2 in range(2):
                    for f in range(8):
                        mm1.append(P.emit("t", lambda t, k2=k2, f=f, bi=bi: t.matmul(
                            pckv[:, 128 * k2:128 * k2 + 128],
                            wdkv[:, f, 128 * k2:128 * k2 + 128],
                            xnT[bi][:, f, :], start=(f == 0), stop=(f == 7)),
                            deps=[dev, dwk["dkv"]] + ([prev_ck] if (k2 == 0 and f == 0) else [])))
                tgt = ckvd if is_q else ckv
                col = 128 * (pos if is_q else it)
                prev_ck = P.emit("v", lambda v, tgt=tgt, col=col: v.tensor_copy(
                    tgt[:, :, col:col + 128], pckv[:].rearrange("p (a k) -> p a k", a=2)),
                    deps=[mm1[-1]])
                mm2 = []
                for f in range(8):
                    mm2.append(P.emit("t", lambda t, f=f, bi=bi: t.matmul(
                        pkr[:], wkr[:, f, :], xnT[bi][:, f, :],
                        start=(f == 0), stop=(f == 7)),
                        deps=[dev, dwk["kr"]] + ([prev_kr] if f == 0 else [])))
                free_xnT[bi] = mm2[-1]
                sc = scqk if is_q else scfull
                ktgt = kRq if is_q else kRf
                prev_kr = P.emit("v", lambda v, ktgt=ktgt, sc=sc, col=col: v.tensor_tensor(
                    ktgt[0:32, col:col + 128], pkr[:], sc[:, col:col + 128], ALU.mult),
                    deps=[mm2[-1], dscf, dscq])
            ln_kr_done = prev_kr
            ln_ck_done = prev_ck
            if "ckv" in taps:
                dbg_ckv = dt("dbg_ckv", [128, 2, T_], BF16, kind="ExternalOutput").ap()
                P.emit("g", P.dma("g", dbg_ckv, ckv[:]), deps=[ln_ck_done], kind="dma")
            if "ckvd" in taps:
                dbg_ckvd = dt("dbg_ckvd", [128, 2, NQ], BF16, kind="ExternalOutput").ap()
                P.emit("g", P.dma("g", dbg_ckvd, ckvd[:]), deps=[ln_ck_done], kind="dma")
            if "kRf" in taps:
                dbg_krf = dt("dbg_krf", [33, T_], BF16, kind="ExternalOutput").ap()
                P.emit("g", P.dma("g", dbg_krf, kRf[:]), deps=[ln_kr_done, mone], kind="dma")
            if "scf" in taps:
                dbg_scf = dt("dbg_scf", [32, T_], F32, kind="ExternalOutput").ap()
                P.emit("g", P.dma("g", dbg_scf, scfull[:]), deps=[dscf], kind="dma")
            if "xnqT" in taps:
                dbg_xnq = dt("dbg_xnq", [128, 8, NQ], BF16, kind="ExternalOutput").ap()
                P.emit("g", P.dma("g", dbg_xnq, xnqT[:]), deps=[xnq_last], kind="dma")
            if "weff" in taps:
                dbg_weff = dt("dbg_weff", [128, 8, 1536], BF16, kind="ExternalOutput").ap()
                P.emit("g", P.dma("g", dbg_weff, weff[:]), deps=[weff_done], kind="dma")
            P.flush(blk)
            c1p.close()
        if upto < 4:
            raise _StopBuild()

        with nc.Block("p1b") as blk:
            e1b = c1.enter_context
            vstg = [e1b(nc.sbuf_tensor(f"vstg{i}", [128, 16, 260], BF16)) for i in range(2)]
            kstg = [e1b(nc.sbuf_tensor(f"kstg{i}", [64, T_], BF16)) for i in range(2)]
            pv = [e1b(nc.psum_tensor(f"pv{i}", [128, 1024], F32)) for i in range(2)]
            pk = [e1b(nc.psum_tensor(f"pk{i}", [64, 512], F32)) for i in range(2)]
            vst_ones = []
            for i in range(2):
                vst_ones.append(P.emit("g", lambda g, i=i: g.memset(
                    vstg[i][:].rearrange("p h (c o) -> p h c o", o=65)[:, :, :, 64:65],
                    1.0)))

            prev_pv = [None, None]
            vb_store = [None, None]

            def vbuild(ckv_src, nchunks, dall, dep_ck):
                nonlocal prev_pv
                last_store = vb_store
                final = None
                for grp in range(nchunks // 4):
                    gb = grp % 2
                    ev_last = None
                    for tc4 in range(4):
                        tc = grp * 4 + tc4
                        pb = tc % 2
                        mms = []
                        for k2 in range(2):
                            for n2 in range(2):
                                mms.append(P.emit("t",
                                    lambda t, k2=k2, n2=n2, tc=tc, pb=pb, ckv_src=ckv_src: t.matmul(
                                        pv[pb][:, 512 * n2:512 * n2 + 512],
                                        ckv_src[:, k2, 128 * tc:128 * tc + 128],
                                        wuv2[:, k2, 512 * n2:512 * n2 + 512],
                                        start=(k2 == 0), stop=(k2 == 1)),
                                    deps=[dep_ck, dwk["uv"], prev_pv[pb]]))
                        ev_last = P.emit("v", lambda v, tc4=tc4, gb=gb, pb=pb: v.tensor_copy(
                            vstg[gb][:].rearrange("p h (c o) -> p h c o", o=65)[:, :, tc4, 0:64],
                            pv[pb][:].rearrange("p (h o) -> p h o", o=64)),
                            deps=[mms[-1], vst_ones[gb], last_store[gb]])
                        prev_pv[pb] = ev_last
                    for h in range(H_):
                        final = P.emit("s", P.dma("s",
                            dall[h, :, grp, :], vstg[gb][:, h, :]),
                            deps=[ev_last], kind="dma")
                    last_store[gb] = final
                return final

            v_done = vbuild(ckv, 32, vall, ln_ck_done)
            vd_done = vbuild(ckvd, 16, vdall, ln_ck_done)

            prev_pk = [None, None]
            kb_store = [None, None]

            def kbuild(ckv_src, ntok, dall, krsrc, dep_ck, dep_kr):
                nonlocal prev_pk
                prev_dma = kb_store
                final = None
                for h in range(H_):
                    hb = h % 2
                    ev_last = None
                    for ns in range(ntok // 512):
                        pb = ns % 2
                        mms = []
                        for k2 in range(2):
                            mms.append(P.emit("t",
                                lambda t, k2=k2, ns=ns, h=h, pb=pb, ckv_src=ckv_src: t.matmul(
                                    pk[pb][:],
                                    wuk2[:, k2, 64 * h:64 * h + 64],
                                    ckv_src[:, k2, 512 * ns:512 * ns + 512],
                                    start=(k2 == 0), stop=(k2 == 1)),
                                deps=[dep_ck, dwk["uk"], prev_pk[pb]]))
                        ev_last = P.emit("v", lambda v, ns=ns, hb=hb, pb=pb: v.tensor_copy(
                            kstg[hb][:, 512 * ns:512 * ns + 512], pk[pb][:]),
                            deps=[mms[-1], prev_dma[hb]])
                        prev_pk[pb] = ev_last
                    d1 = P.emit("s", P.dma("s", dall[h, 0:64, 0:ntok], kstg[hb][:, 0:ntok]),
                                deps=[ev_last], kind="dma")
                    d2 = P.emit("s", P.dma("s", dall[h, 64:97, 0:ntok], krsrc[:, 0:ntok]),
                                deps=list(dep_kr), kind="dma")
                    prev_dma[hb] = d1
                    final = d2
                return final

            k_done = kbuild(ckv, T_, kall, kRf, ln_ck_done, [ln_kr_done, mone])
            kd_done = kbuild(ckvd, NQ, kdall, kRq, ln_ck_done, [ln_kr_done, mone2])
            P.flush(blk)
        c1.close()
        kv_scratch_done = [v_done, vd_done, k_done, kd_done]
        if upto < 5:
            raise _StopBuild()

        qall_stores_all = []
        ffn_stores = []
        with ExitStack() as c2, nc.Block("p2") as blk:
            e2 = c2.enter_context
            qstg = e2(nc.sbuf_tensor("qstg", [97, 16, 512], BF16))
            actblk = e2(nc.sbuf_tensor("actblk", [128, 32, 512], BF16))
            w1b = [e2(nc.sbuf_tensor(f"w1b{i}", [128, 8, 256], BF16)) for i in range(2)]
            w2b = [e2(nc.sbuf_tensor(f"w2b{i}", [128, 2, 1024], BF16)) for i in range(2)]
            scrA_ = [e2(nc.sbuf_tensor(f"scrA{i}", [128, 512], F32)) for i in range(2)]
            scrB_ = [e2(nc.sbuf_tensor(f"scrB{i}", [128, 512], F32)) for i in range(2)]
            scrI = e2(nc.sbuf_tensor("scrI", [128, 512], I32))
            scrC = e2(nc.sbuf_tensor("scrC", [128, 512], F32))
            otile = [e2(nc.sbuf_tensor(f"otile{i}", [128, 1024], F32)) for i in range(2)]
            pq = [e2(nc.psum_tensor(f"pq{i}", [128, 512], F32)) for i in range(2)]
            pw2 = [e2(nc.psum_tensor(f"pw2{i}", [128, 1024], F32)) for i in range(2)]

            if "qchain" in taps:
                dbgq = [dt(f"dbgq{i}", [128, 512], F32, kind="ExternalOutput").ap() for i in range(5)]
                dbgq8 = dt("dbgq8", [128, 512], BF16, kind="ExternalOutput").ap()
            last_pq = [weff_done, weff_done]
            last_w1l = [None, None]
            last_w2l = [None, None]
            prev_u = [None, None]
            prev_u2 = [None, None]
            qall_prev = []
            ffn_st = [None, None]
            ffn_ev = [None, None]
            prev_d7 = [None, None]
            last_w2ev = [None, None]
            mq_prev = None
            for bl in range(4):
                cols = slice(512 * bl, 512 * bl + 512)
                mq96 = P.emit("g", lambda g: g.memset(qstg[96:97, :, :], -10.0),
                              deps=qall_prev + [mq_prev])
                mq_prev = mq96
                st16 = []
                for qc in range(12):
                    qb = qc % 2
                    mms = []
                    for f in range(8):
                        mms.append(P.emit("t", lambda t, f=f, qc=qc, qb=qb, cols=cols: t.matmul(
                            pq[qb][:], weff[:, f, 128 * qc:128 * qc + 128],
                            xnqT[:, f, cols], start=(f == 0), stop=(f == 7)),
                            deps=[weff_done, xnq_last, last_pq[qb]]))
                    if qc < 8:
                        P.emit("v", lambda v, qc=qc, qb=qb: v.tensor_scalar(
                            qstg[0:64, 2 * qc, :], pq[qb][0:64, :], INV_SQ96, None, ALU.mult),
                            deps=[mms[-1]] + qall_prev)
                        last_pq[qb] = P.emit("v", lambda v, qc=qc, qb=qb: v.tensor_scalar(
                            qstg[0:64, 2 * qc + 1, :], pq[qb][64:128, :], INV_SQ96, None,
                            ALU.mult), deps=[mms[-1]])
                    else:
                        evl = None
                        for m in range(4):
                            h = 4 * (qc - 8) + m
                            evl = P.emit("v", lambda v, m=m, h=h, qb=qb, cols=cols: v.tensor_tensor(
                                qstg[64:96, h, :], pq[qb][32 * m:32 * m + 32, :],
                                scqrep[32 * m:32 * m + 32, cols], ALU.mult),
                                deps=[mms[-1], dreps[3]] + qall_prev)
                        last_pq[qb] = evl
                for h in range(H_):
                    st16.append(P.emit("s", P.dma("s",
                        qall[h, :, cols], qstg[:, h, :]),
                        deps=[last_pq[0], last_pq[1], mq96], kind="dma"))
                qall_prev = st16
                qall_stores_all += st16

                if p2sub < 2:
                    continue
                for op in range(16):
                    wb = op % 2
                    dlw = P.emit("y", P.dma("y", w1b[wb][:],
                        w1_s[:, 256 * op:256 * op + 256].rearrange("(f p) k -> p f k", p=128)),
                        deps=[last_w1l[wb]] + (w_stage_done if bl == 0 and op < 2 else []),
                        kind="dma")
                    last_w1l[wb] = None
                    for o2 in range(2):
                        o = 2 * op + o2
                        hb = o % 2
                        mms = []
                        for f in range(8):
                            mms.append(P.emit("t", lambda t, f=f, o2=o2, wb=wb, hb=hb, cols=cols: t.matmul(
                                pq[hb][:], w1b[wb][:, f, 128 * o2:128 * o2 + 128],
                                xnqT[:, f, cols], start=(f == 0), stop=(f == 7)),
                                deps=[dlw, xnq_last, last_pq[hb]]))
                        if p2sub == 21:
                            d7 = P.emit("s", lambda s, o=o, hb=hb: s.activation(
                                actblk[:, o, :], pq[hb][:], AF.Relu, bias=b1c[:, o:o + 1]),
                                deps=[mms[-1]])
                            last_pq[hb] = d7
                            prev_d7 = d7
                            last_w1l[wb] = mms[-1]
                            continue
                        sA, sB = scrA_[hb], scrB_[hb]
                        dc_ = P.emit("s", lambda s, o=o, hb=hb, sA=sA: s.activation(
                            sA[:], pq[hb][:], AF.Sin, scale=0.1,
                            bias=cosb[:, o:o + 1]), deps=[mms[-1], s_cosb, prev_d7[hb]])
                        ds_ = P.emit("s", lambda s, o=o, hb=hb, sB=sB: s.activation(
                            sB[:], pq[hb][:], AF.Sin, scale=0.1,
                            bias=sinb[:, o:o + 1]), deps=[mms[-1], prev_u2[hb]])
                        du = P.emit("v", lambda v, o=o, sA=sA: v.tensor_scalar(
                            sA[:], sA[:], ksin[:, o:o + 1], None, ALU.mult),
                            deps=[dc_])
                        du2 = P.emit("v", lambda v, o=o, sA=sA, sB=sB: v.scalar_tensor_tensor(
                            sB[:], sB[:], kcos[:, o:o + 1], sA[:],
                            ALU.mult, ALU.add), deps=[ds_, du])
                        prev_u[hb] = du2
                        d6 = P.emit("v", lambda v, hb=hb, sA=sA, sB=sB: v.tensor_tensor(
                            sA[:], sB[:], pq[hb][:], ALU.add), deps=[du2])
                        prev_u2[hb] = d6
                        last_pq[hb] = d6
                        d7 = P.emit("s", lambda s, o=o, sA=sA: s.activation(
                            actblk[:, o, :], sA[:], AF.Relu, bias=b1c[:, o:o + 1]),
                            deps=[d6] + ([last_w2ev[0], last_w2ev[1]] if op == 0 and o2 == 0 else []))
                        prev_d7[hb] = d7
                        last_w1l[wb] = mms[-1]
                w1_done = P.emit("v", lambda v: v.nop(), deps=[prev_d7[0], prev_d7[1]])
                if bl == 0 and "act0" in taps:
                    dbg_act0 = dt("dbg_act0", [128, 32, 512], BF16, kind="ExternalOutput").ap()
                    P.emit("g", P.dma("g", dbg_act0, actblk[:]), deps=[w1_done], kind="dma")

                if p2sub < 3:
                    continue
                for tp in range(2):
                    mmw = None
                    for op in range(16):
                        wb = op % 2
                        dlw = P.emit("y", P.dma("y", w2b[wb][:],
                            w2_s[256 * op:256 * op + 256, :].rearrange("(a p) n -> p a n", p=128)),
                            deps=[last_w2l[wb]] + (w_stage_done if bl == 0 and tp == 0 and op < 2 else []),
                            kind="dma")
                        for o2 in range(2):
                            o = 2 * op + o2
                            for ti in range(2):
                                ts4 = 2 * tp + ti
                                for n2 in range(2):
                                    mmw = P.emit("t",
                                        lambda t, o=o, o2=o2, ts4=ts4, ti=ti, n2=n2, wb=wb: t.matmul(
                                            pw2[ti][:, 512 * n2:512 * n2 + 512],
                                            actblk[:, o, 128 * ts4:128 * ts4 + 128],
                                            w2b[wb][:, o2, 512 * n2:512 * n2 + 512],
                                            start=(o == 0), stop=(o == 31)),
                                        deps=[dlw, w1_done]
                                        + ([last_w2ev[ti]] if o == 0 else []))
                        last_w2l[wb] = mmw
                    for ti in range(2):
                        ts4 = 2 * tp + ti
                        ob = ts4 % 2
                        dev = P.emit("v", lambda v, ti=ti, ob=ob: v.tensor_tensor(
                            otile[ob][:], pw2[ti][:], b2rep[:], ALU.add),
                            deps=[mmw, s_b2, ffn_st[ob]])
                        last_w2ev[ti] = dev
                        ffn_st[ob] = P.emit("y", P.dma("y",
                            ffn_s[512 * bl + 128 * ts4:512 * bl + 128 * ts4 + 128, :],
                            otile[ob][:]), deps=[dev], kind="dma")
                        ffn_stores.append(ffn_st[ob])
            if "act" in taps:
                dbg_act = dt("dbg_act", [128, 32, 512], BF16, kind="ExternalOutput").ap()
                P.emit("g", P.dma("g", dbg_act, actblk[:]), deps=[prev_d7], kind="dma")
            P.flush(blk)
        ctx12.close()
        if upto < 6:
            raise _StopBuild()

        ctx34 = ExitStack()
        e34 = ctx34.enter_context
        ycache = [e34(nc.sbuf_tensor(f"yc{j}", [128, NQ], F32)) for j in range(8)]

        with ExitStack() as c3, nc.Block("p3") as blk:
            e3 = c3.enter_context
            kh = [e3(nc.sbuf_tensor(f"kh{i}", [97, T_], BF16)) for i in range(2)]
            kdh = [e3(nc.sbuf_tensor(f"kdh{i}", [97, NQ], BF16)) for i in range(2)]
            vh = [e3(nc.sbuf_tensor(f"vh{i}", [128, 2144], BF16)) for i in range(2)]
            vdh = [e3(nc.sbuf_tensor(f"vdh{i}", [128, 1104], BF16)) for i in range(2)]
            qph = [e3(nc.sbuf_tensor(f"qph{i}", [97, NQ], BF16)) for i in range(2)]
            ptb = [e3(nc.sbuf_tensor(f"ptb{i}", [128, 1024], BF16)) for i in range(2)]
            dtmp = [e3(nc.sbuf_tensor(f"dtmp{i}", [1, NQ], F32)) for i in range(2)]
            ztile = e3(nc.sbuf_tensor("ztile", [128, 128], BF16))
            psc = [e3(nc.psum_tensor(f"psc{i}", [128, 1024], F32)) for i in range(2)]
            psy = e3(nc.psum_tensor("psy", [128, NQ], F32))

            zm = P.emit("g", lambda g: g.memset(ztile[:], 0.0))
            zc = P.gchk([zm])
            warm = None
            for p_ in range(NPOS):
                warm = P.emit("t", lambda t, p_=p_: t.matmul(
                    psy[:, 128 * p_:128 * p_ + 128], ztile[:], ztile[:],
                    start=True, stop=True), deps=[zc])
            HEAD_ORDER = [0] + list(range(H_))
            NH = len(HEAD_ORDER)
            head_done = [None] * NH
            yevac = [None] * NH
            denev = [None] * NH
            denst = [None] * NH
            for hi, h in enumerate(HEAD_ORDER):
                hb = hi % 2
                depf = head_done[hi - 2] if hi >= 2 else None
                dep0 = (kv_scratch_done + qall_stores_all) if hi == 0 else []
                l1 = P.emit("y", P.dma("y", kh[hb][:], kall[h, :, :]),
                            deps=dep0 + [depf], kind="dma")
                l2 = P.emit("y", P.dma("y", kdh[hb][:], kdall[h, :, :]),
                            deps=[depf], kind="dma")
                l3 = P.emit("y", P.dma("y",
                                       vh[hb][:, 0:2080].rearrange("p (g c) -> p g c", c=260),
                                       vall[h]),
                            deps=[depf], kind="dma")
                l4 = P.emit("y", P.dma("y",
                                       vdh[hb][:, 0:1040].rearrange("p (g c) -> p g c", c=260),
                                       vdall[h]),
                            deps=[depf], kind="dma")
                l5 = P.emit("y", P.dma("y", qph[hb][:], qall[h, :, :]),
                            deps=[depf], kind="dma")

                if hi == 1 and "p3in" in taps:
                    dbg_kh = dt("dbg_kh", [97, T_], BF16, kind="ExternalOutput").ap()
                    dbg_qh = dt("dbg_qh", [97, NQ], BF16, kind="ExternalOutput").ap()
                    dbg_vh = dt("dbg_vh", [128, 8, 260], BF16, kind="ExternalOutput").ap()
                    t1_ = P.emit("s", P.dma("s", dbg_kh, kh[0][:]), deps=[l1], kind="dma")
                    t2_ = P.emit("s", P.dma("s", dbg_qh, qph[0][:]), deps=[l5], kind="dma")
                    t3_ = P.emit("s", P.dma("s", dbg_vh, vh[0][:]), deps=[l3], kind="dma")
                    l1 = P.emit("g", lambda g: g.nop(), deps=[t1_, t2_, t3_])
                    l2 = l3 = l4 = l5 = l1
                exp_s = [None] * NGRP
                av_s = [None] * NGRP

                def emit_scores(gi):
                    kind, c, p0, ln, pad_abs = GROUPS[gi]
                    gb = gi % 2
                    mms = []
                    dep_psc = exp_s[gi - 2] if gi >= 2 else None
                    if kind == 0:
                        lhs = kh[hb][:, 128 * c:128 * c + 128]
                        for q0 in range(0, ln, 4):
                            w = min(4, ln - q0)
                            mms.append(P.emit("t",
                                lambda t, lhs=lhs, q0=q0, w=w, p0=p0, gb=gb, hb=hb: t.matmul(
                                    psc[gb][:, 128 * q0:128 * q0 + 128 * w], lhs,
                                    qph[hb][:, 128 * (p0 + q0):128 * (p0 + q0) + 128 * w],
                                    start=True, stop=True),
                                deps=[l1, l5, dep_psc]))
                    else:
                        for sl in range(ln):
                            p = p0 + sl
                            lhs = kdh[hb][:, 128 * p:128 * p + 128]
                            mms.append(P.emit("t", lambda t, lhs=lhs, p=p, sl=sl, gb=gb, hb=hb: t.matmul(
                                psc[gb][:, 128 * sl:128 * sl + 128], lhs,
                                qph[hb][:, 128 * p:128 * p + 128],
                                start=(sl % 4 == 0), stop=True),
                                deps=[l2, l5, dep_psc]))
                    vdep = None
                    if kind == 1:
                        vdep = P.emit("v", lambda v, gb=gb: v.tensor_tensor(
                            psc[gb][:], psc[gb][:], tri8b[:], ALU.add),
                            deps=[mms[-1], s_tri])
                    elif pad_abs is not None:
                        sl_loc = pad_abs - p0
                        vdep = P.emit("v", lambda v, gb=gb, sl_loc=sl_loc, pad_abs=pad_abs: v.tensor_scalar(
                            psc[gb][:, 128 * sl_loc:128 * sl_loc + 128],
                            psc[gb][:, 128 * sl_loc:128 * sl_loc + 128],
                            padb[:, pad_abs:pad_abs + 1], None, ALU.add),
                            deps=[mms[-1], s_pad])
                    dep_pt = av_s[gi - 2] if gi >= 2 else None
                    exp_s[gi] = P.emit("s", lambda s, gb=gb, ln=ln: s.activation(
                        ptb[gb][:, 0:128 * ln], psc[gb][:, 0:128 * ln], AF.Exp),
                        deps=[mms[-1], vdep, dep_pt])

                def emit_attv(gi):
                    kind, c, p0, ln, pad_abs = GROUPS[gi]
                    gb = gi % 2
                    avs = []
                    if kind == 0:
                        off = 260 * (c // 4) + 65 * (c % 4)
                        vap = vh[hb][:, off:off + 128][:, 0:65]
                        for q0 in range(0, ln, 4):
                            w = min(4, ln - q0)
                            first = (c == 0)
                            avs.append(P.emit("t",
                                lambda t, vap=vap, q0=q0, w=w, p0=p0, gb=gb, first=first: t.matmul(
                                    psy[0:65, 128 * (p0 + q0):128 * (p0 + q0) + 128 * w], vap,
                                    ptb[gb][:, 128 * q0:128 * q0 + 128 * w],
                                    start=first, stop=False),
                                deps=[exp_s[gi], l3, warm]
                                + ([yevac[hi - 1], denev[hi - 1]] if (gi == 0 and hi >= 1) else [])))
                    else:
                        for sl in range(ln):
                            p = p0 + sl
                            off = 260 * (p // 4) + 65 * (p % 4)
                            vap = vdh[hb][:, off:off + 128][:, 0:65]
                            avs.append(P.emit("t",
                                lambda t, vap=vap, p=p, sl=sl, gb=gb: t.matmul(
                                    psy[0:65, 128 * p:128 * p + 128], vap,
                                    ptb[gb][:, 128 * sl:128 * sl + 128],
                                    start=False, stop=True),
                                deps=[exp_s[gi], l4, warm]))
                    av_s[gi] = avs[-1]

                for gi in range(NGRP):
                    emit_scores(gi)
                    if gi >= 1:
                        emit_attv(gi - 1)
                emit_attv(NGRP - 1)
                head_done[hi] = av_s[-1]
                yevac[hi] = P.emit("v", lambda v, h=h: v.tensor_copy(
                    ycache[h // 2][64 * (h % 2):64 * (h % 2) + 64, :], psy[0:64, :]),
                    deps=[av_s[-1]])
                denev[hi] = P.emit("v", lambda v, h=h, hb=hb: v.tensor_copy(
                    dtmp[hb][:], psy[64:65, :]),
                    deps=[av_s[-1], denst[hi - 2] if hi >= 2 else None])
                denst[hi] = P.emit("s", P.dma("s", denh_s[h:h + 1, :], dtmp[hb][:]),
                                  deps=[denev[hi]], kind="dma")
            if "yc" in taps:
                dbg_yc = dt("dbg_yc", [128, 8, NQ], F32, kind="ExternalOutput").ap()
                for j in range(8):
                    P.emit("s", P.dma("s", dbg_yc[:, j, :], ycache[j][:]),
                           deps=list(yevac), kind="dma")
            P.flush(blk)

        if upto < 7:
            raise _StopBuild()
        with ExitStack() as c4, nc.Block("p4") as blk:
            e4 = c4.enter_context
            ybf = e4(nc.sbuf_tensor("ybf", [128, 8, NQ], BF16))
            wob = e4(nc.sbuf_tensor("wob", [128, 8, C_], BF16))
            dn128 = e4(nc.sbuf_tensor("dn128", [128, 256], F32))
            rrep = [e4(nc.sbuf_tensor("rrep0", [128, NQ], F32))] * 2
            xrt = [e4(nc.sbuf_tensor(f"xrt{i}", [128, C_], F32)) for i in range(2)]
            frt = [e4(nc.sbuf_tensor(f"frt{i}", [128, C_], F32)) for i in range(2)]
            ott = [e4(nc.sbuf_tensor(f"ott{i}", [128, C_], F32)) for i in range(2)]
            pwo = [e4(nc.psum_tensor(f"pwo{i}", [128, 1024], F32)) for i in range(2)]

            dwo = P.emit("g", P.dma("g", wob[:],
                                    W_O.rearrange("(j p) n -> p j n", p=128)), kind="dma")
            dwo = P.gchk([dwo])
            dr1 = P.emit("y", P.dma("y", dn128[:],
                                    denh_s.rearrange("h (a n) -> h a n", n=256)),
                         deps=list(denst), kind="dma")
            dr2 = P.emit("v", lambda v: v.reciprocal(dn128[:], dn128[:]), deps=[dr1])
            dr3 = P.emit("y", P.dma("y", denr_s.rearrange("h (a n) -> h a n", n=256),
                                    dn128[:]), deps=[dr2], kind="dma")
            norm = [None] * H_
            repl = [None, None]
            for h in range(H_):
                rb = h % 2
                repl[rb] = P.emit("y", P.dma("y", rrep[rb][:],
                                             denr_s[h:h + 1, :].partition_broadcast(128)),
                                  deps=[dr3, norm[h - 1] if h >= 1 else None], kind="dma")
                norm[h] = P.emit("v", lambda v, h=h, rb=rb: v.tensor_tensor(
                    ybf[64 * (h % 2):64 * (h % 2) + 64, h // 2, :],
                    ycache[h // 2][64 * (h % 2):64 * (h % 2) + 64, :],
                    rrep[rb][64 * (h % 2):64 * (h % 2) + 64, :], ALU.mult),
                    deps=[repl[rb], yevac[h + 1]])

            ost = [None, None]
            oev = [None, None]
            for pos in range(NPOS):
                pb = pos % 2
                dxr = P.emit("s", P.dma("s", xrt[pb][:], xq[128 * pos:128 * pos + 128, :]),
                             deps=[oev[pb]], kind="dma")
                dfr = P.emit("s", P.dma("s", frt[pb][:], ffn_s[128 * pos:128 * pos + 128, :]),
                             deps=[oev[pb]] + (ffn_stores if pos == 0 else []), kind="dma")
                mms = []
                for j in range(8):
                    for n2 in range(2):
                        mms.append(P.emit("t", lambda t, j=j, n2=n2, pos=pos, pb=pb: t.matmul(
                            pwo[pb][:, 512 * n2:512 * n2 + 512],
                            ybf[:, j, 128 * pos:128 * pos + 128],
                            wob[:, j, 512 * n2:512 * n2 + 512],
                            start=(j == 0), stop=(j == 7)),
                            deps=[dwo, norm[2 * j], norm[2 * j + 1]]
                            + ([oev[pb]] if j == 0 else [])))
                d1 = P.emit("v", lambda v, pb=pb: v.tensor_tensor(
                    ott[pb][:], pwo[pb][:], xrt[pb][:], ALU.add),
                    deps=[mms[-1], dxr, ost[pb]])
                d2 = P.emit("v", lambda v, pb=pb: v.tensor_tensor(
                    ott[pb][:], ott[pb][:], frt[pb][:], ALU.add), deps=[d1, dfr])
                oev[pb] = d2
                ost[pb] = P.emit("y", P.dma("y", out[128 * pos:128 * pos + 128, :],
                                            ott[pb][:]), deps=[d2], kind="dma")
            P.flush(blk)
        ctx34.close()

      _ = None
    except _StopBuild:
        pass
    return nc


_NC_CACHE = None


def _get_nc():
    global _NC_CACHE
    if _NC_CACHE is None:
        _NC_CACHE = build_nc()
    return _NC_CACHE


def _host_consts():
    c = {}
    c["identf"] = np.eye(128, dtype=np.float32)
    r = np.arange(128)
    tri = np.where(r[None, :] >= r[:, None], 0.0, NEG_BIG).astype(np.float32)
    c["tri8"] = np.ascontiguousarray(np.tile(tri, (1, 8)).astype(np.float32))
    c["trowf"] = np.ascontiguousarray(np.tile(np.arange(T_, dtype=np.float32), (32, 1)))
    c["jcol"] = ((np.arange(32) % 16) / 16.0).astype(np.float32).reshape(32, 1)
    return c


def kernel(L0_attn_x, L0_attn_W_DQ, L0_attn_W_UQ, L0_attn_W_DKV, L0_attn_W_UK,
           L0_attn_W_UV, L0_attn_W_QR, L0_attn_W_KR, L0_attn_W_O,
           L0_ffn_W1, L0_ffn_b1, L0_ffn_quantum_phases,
           L0_ffn_quantum_weights_real, L0_ffn_quantum_weights_imag,
           L0_ffn_W2, L0_ffn_b2):
    x = np.ascontiguousarray(np.asarray(L0_attn_x, dtype=np.float32))
    consts = _host_consts()
    shared = dict(
        W_DQ=np.ascontiguousarray(L0_attn_W_DQ, dtype=np.float32),
        W_UQ=np.ascontiguousarray(L0_attn_W_UQ, dtype=np.float32),
        W_QR=np.ascontiguousarray(L0_attn_W_QR, dtype=np.float32),
        W_DKV=np.ascontiguousarray(L0_attn_W_DKV, dtype=np.float32),
        W_UK=np.ascontiguousarray(L0_attn_W_UK, dtype=np.float32),
        W_UV=np.ascontiguousarray(L0_attn_W_UV, dtype=np.float32),
        WKRpt=np.ascontiguousarray(
            np.asarray(L0_attn_W_KR, dtype=np.float32).reshape(8, 128, 32).transpose(1, 0, 2)),
        W_O=np.ascontiguousarray(L0_attn_W_O, dtype=np.float32),
        W1=np.ascontiguousarray(L0_ffn_W1, dtype=np.float32),
        W2=np.ascontiguousarray(L0_ffn_W2, dtype=np.float32),
        b1pt=np.ascontiguousarray(np.asarray(L0_ffn_b1, dtype=np.float32).reshape(32, 128).T),
        thpt=np.ascontiguousarray(
            np.asarray(L0_ffn_quantum_phases, dtype=np.float32).reshape(32, 128).T),
        wrpt=np.ascontiguousarray(
            np.asarray(L0_ffn_quantum_weights_real, dtype=np.float32).reshape(32, 128).T),
        wipt=np.ascontiguousarray(
            np.asarray(L0_ffn_quantum_weights_imag, dtype=np.float32).reshape(32, 128).T),
        b2row=np.ascontiguousarray(np.asarray(L0_ffn_b2, dtype=np.float32).reshape(1, C_)),
        **consts,
    )
    in_maps = []
    qrows_all = []
    for core in range(8):
        b = core // 2
        tiles = TILES_A if core % 2 == 0 else TILES_B
        qrows = np.concatenate([np.arange(128 * j, 128 * j + 128) for j in tiles])
        qrows_all.append(qrows)
        padv = np.array([0.0 if tiles[p] == C_P[p] - 1 else NEG_BIG
                         for p in range(NPOS)], dtype=np.float32)
        im = dict(shared)
        im["xfull"] = np.ascontiguousarray(x[b])
        im["xq"] = np.ascontiguousarray(x[b][qrows])
        im["padbin"] = np.ascontiguousarray(np.tile(padv[None, :], (128, 1)).astype(np.float32))
        im["trowq"] = np.ascontiguousarray(np.tile(qrows.astype(np.float32)[None, :], (32, 1)))
        in_maps.append(im)
    nc = _get_nc()
    res = run_bass_kernel_spmd(nc, in_maps, core_ids=list(range(8)))
    full = np.empty((B_, T_, C_), dtype=np.float32)
    for core in range(8):
        full[core // 2][qrows_all[core]] = np.asarray(res.results[core]["out"])
    return full
